# version 42
# baseline (speedup 1.0000x reference)
import math
import numpy as np
import ml_dtypes
from contextlib import ExitStack

import concourse.bass as bass
import concourse.mybir as mybir
from concourse.bass_utils import run_bass_kernel_spmd

F32 = mybir.dt.float32
BF16 = mybir.dt.bfloat16
AF = mybir.ActivationFunctionType
ALU = mybir.AluOpType
AX = mybir.AxisListType

T = 2048
C = 1024
H = 16
HS = 64
DHR = 32
DQK = HS + DHR
DC = 4096
DKV = 256
NT = T // 128
INV_SQRT = 1.0 / math.sqrt(DQK)

ENGINES = ("tensor", "vector", "scalar", "gpsimd", "sync")


class Chan:
    def __init__(self, sem):
        self.sem = sem
        self.n = 0


class Builder:

    def __init__(self):
        self.plan = {e: [] for e in ENGINES}
        self.lastw = {e: {} for e in ENGINES}

    def wait(self, eng, ch, val=None):
        v = ch.n if val is None else val
        if v <= 0:
            return
        if self.lastw[eng].get(id(ch), -1) >= v:
            return
        self.lastw[eng][id(ch)] = v
        self.plan[eng].append(lambda e, s=ch.sem, vv=v: e.wait_ge(s, vv))

    def op(self, eng, fn, inc=None, k=1):
        if inc is not None:
            inc.n += k
            self.plan[eng].append(
                lambda e, f=fn, s=inc.sem, kk=k: f(e).then_inc(s, kk))
        else:
            self.plan[eng].append(fn)

    def dma(self, eng, fn, ch):
        self.op(eng, fn, inc=ch, k=16)

    def barrier(self, chans):
        snap = [(ch, ch.n) for ch in chans]
        for eng in ENGINES:
            for ch, v in snap:
                self.wait(eng, ch, v)


def build_program(debug=False, phases=3):
    nc = bass.Bass("TRN2", target_bir_lowering=False)

    x_hbm = nc.dram_tensor("x", [T, C], F32, kind="ExternalInput").ap()
    w_dq = nc.dram_tensor("W_DQ", [C, DC], F32, kind="ExternalInput").ap()
    w_uq = nc.dram_tensor("W_UQ", [DC, C], F32, kind="ExternalInput").ap()
    w_dkv = nc.dram_tensor("W_DKV", [C, DKV], F32, kind="ExternalInput").ap()
    w_uk = nc.dram_tensor("W_UK", [DKV, C], F32, kind="ExternalInput").ap()
    w_uv = nc.dram_tensor("W_UV", [DKV, C], F32, kind="ExternalInput").ap()
    w_qr = nc.dram_tensor("W_QR", [DC, H * DHR], F32, kind="ExternalInput").ap()
    w_kr = nc.dram_tensor("W_KR", [C, DHR], F32, kind="ExternalInput").ap()
    w_o = nc.dram_tensor("W_O", [C, C], F32, kind="ExternalInput").ap()
    cbt = nc.dram_tensor("CBT", [C, C], F32, kind="ExternalInput").ap()
    scb_h = nc.dram_tensor("SCB", [1, C], F32, kind="ExternalInput").ap()
    bias_h = nc.dram_tensor("BIAS", [1, C], F32, kind="ExternalInput").ap()
    scale_h = nc.dram_tensor("ROPE", [128, T], BF16, kind="ExternalInput").ap()
    masks_h = nc.dram_tensor("MASKS", [128, 2560], BF16, kind="ExternalInput").ap()
    c127_h = nc.dram_tensor("C127", [1, 128], F32, kind="ExternalInput").ap()
    ones_h = nc.dram_tensor("ONES", [1, 128], F32, kind="ExternalInput").ap()
    onesbf_h = nc.dram_tensor("ONESBF", [1, 128], BF16, kind="ExternalInput").ap()
    ident_h = nc.dram_tensor("IDENT", [128, 128], BF16, kind="ExternalInput").ap()
    y_hbm = nc.dram_tensor("y", [T, C], F32, kind="ExternalOutput").ap()
    qt_kind = "ExternalOutput" if debug else "Internal"
    qt_hbm = nc.dram_tensor("qt", [H, DQK, T], BF16, kind=qt_kind).ap()
    dbg = {}
    if debug:
        dbg["xnT"] = nc.dram_tensor("d_xnT", [128, 16384], BF16, kind="ExternalOutput").ap()
        dbg["ckvT"] = nc.dram_tensor("d_ckvT", [128, 4096], BF16, kind="ExternalOutput").ap()
        dbg["krT"] = nc.dram_tensor("d_krT", [32, 2048], BF16, kind="ExternalOutput").ap()
        dbg["yT"] = nc.dram_tensor("d_yT", [128, 16384], BF16, kind="ExternalOutput").ap()
        dbg["ktb0"] = nc.dram_tensor("d_ktb0", [96, 2048], BF16, kind="ExternalOutput").ap()
        dbg["wdqT"] = nc.dram_tensor("d_wdqT", [128, 32768], BF16, kind="ExternalOutput").ap()
        dbg["weff"] = nc.dram_tensor("d_weff", [128, 12288], BF16, kind="ExternalOutput").ap()
        dbg["ktb1"] = nc.dram_tensor("d_ktb1", [96, 2048], BF16, kind="ExternalOutput").ap()
        dbg["vb0"] = nc.dram_tensor("d_vb0", [128, 2048], BF16, kind="ExternalOutput").ap()
        dbg["vb1"] = nc.dram_tensor("d_vb1", [128, 2048], BF16, kind="ExternalOutput").ap()

    b = Builder()

    with ExitStack() as ctx:
        def chan(name):
            return Chan(ctx.enter_context(nc.semaphore(name)))
        wq = chan("wq")
        sq = chan("sq")
        xq = chan("xq")
        pq = chan("pq")
        vqA = chan("vqA")
        vqB = chan("vqB")
        vqC = chan("vqC")
        vqD = chan("vqD")
        aq = chan("aq")
        st = chan("st")
        pe = chan("pe")
        act = chan("act")
        dve = chan("dve")
        pool = chan("pool")
        ALLCH = [wq, sq, xq, pq, vqA, vqB, vqC, vqD, aq, st, pe, act, dve, pool]

        block = ctx.enter_context(nc.Block())

        def flush():
            for eng in ENGINES:
                fns = b.plan[eng]
                if not fns:
                    continue
                b.plan[eng] = []

                def section(e, fns=fns):
                    for f in fns:
                        f(e)
                getattr(block, eng)(section)

        xnT = ctx.enter_context(nc.sbuf_tensor("xnT", [128, 16384], BF16))
        ckvT = ctx.enter_context(nc.sbuf_tensor("ckvT", [128, 4096], BF16))
        krT = ctx.enter_context(nc.sbuf_tensor("krT", [32, 2048], BF16))
        rope = ctx.enter_context(nc.sbuf_tensor("rope", [128, 2048], BF16))
        c127 = ctx.enter_context(nc.sbuf_tensor("c127", [1, 128], F32))
        onesbf = ctx.enter_context(nc.sbuf_tensor("onesbf", [1, 128], BF16))
        ident = ctx.enter_context(nc.sbuf_tensor("ident", [128, 128], BF16))
        x_sb = ctx.enter_context(nc.sbuf_tensor("x_sb", [128, 4096], F32))
        ktb = [None, None, None]
        vb = [ctx.enter_context(nc.sbuf_tensor("vb0", [128, 2048], BF16)),
              ctx.enter_context(nc.sbuf_tensor("vb1", [128, 2048], BF16)),
              None]
        vstage = ctx.enter_context(nc.sbuf_tensor("vstage", [128, 2048], BF16))

        xld_at = {}
        for g0 in range(3):
            b.dma("sync", lambda e, g0=g0: e.dma_start(
                out=x_sb[:, g0 * 1024:(g0 + 1) * 1024],
                in_=x_hbm[g0 * 128:(g0 + 1) * 128, :]), xq)
            xld_at[g0] = xq.n

        b.dma("sync", lambda e: e.dma_start(out=c127[:], in_=c127_h), sq)
        b.dma("sync", lambda e: e.dma_start(out=onesbf[:], in_=onesbf_h), sq)
        b.dma("sync", lambda e: e.dma_start(out=ident[:], in_=ident_h), sq)
        ident_sq = sq.n
        b.dma("sync", lambda e: e.dma_start(out=rope[:], in_=scale_h), sq)
        const_sq = sq.n

        kt_rd = [[0] * 4 for _ in range(3)]
        vb_rd = [[[] for _ in range(16)] for _ in range(3)]
        vstage_free = [[], [], [], []]
        uk_ld = [0]
        ckv_rdy = [0]

        def emit_build_tile(p, k, prev_pe, psy, psy_free, wuk, wuv, wready):
            bank = k % 2
            bufA, bufB = (2 * p) % 3, (2 * p + 1) % 3
            b.wait("tensor", wready[0], wready[1])
            b.wait("tensor", dve, ckv_rdy[0])
            b.wait("tensor", pe, prev_pe)
            b.wait("tensor", dve, psy_free[bank])
            if k >= 4:
                t4 = k - 4
                for c in range(2):
                    b.op("tensor", lambda e, bank=bank, c=c, p=p, t4=t4, psy=psy, wuk=wuk: e.matmul(
                        psy(bank),
                        wuk[:, c * 1024 + 128 * p:c * 1024 + 128 * p + 128],
                        ckvT[:, c * 2048 + t4 * 512:c * 2048 + t4 * 512 + 512],
                        start=(c == 0), stop=(c == 1)),
                        inc=(pe if c == 1 else None))
                mmn = pe.n
                b.wait("vector", pe, mmn)
                b.op("vector", lambda e, bank=bank, bufA=bufA, t4=t4, psy=psy: e.tensor_copy(
                    out=ktb[bufA][0:64, t4 * 512:(t4 + 1) * 512],
                    in_=psy(bank)[0:64, :]), inc=dve)
                b.op("vector", lambda e, bank=bank, bufB=bufB, t4=t4, psy=psy: e.tensor_copy(
                    out=ktb[bufB][0:64, t4 * 512:(t4 + 1) * 512],
                    in_=psy(bank)[64:128, :]), inc=dve)
                psy_free[bank] = dve.n
                kt_rd[bufA][t4] = kt_rd[bufB][t4] = dve.n
            else:
                t4 = k
                for c in range(2):
                    b.op("tensor", lambda e, bank=bank, c=c, p=p, t4=t4, psy=psy, wuv=wuv: e.matmul(
                        psy(bank),
                        wuv[:, c * 1024 + 128 * p:c * 1024 + 128 * p + 128],
                        ckvT[:, c * 2048 + t4 * 512:c * 2048 + t4 * 512 + 512],
                        start=(c == 0), stop=(c == 1)),
                        inc=(pe if c == 1 else None))
                mmn = pe.n
                b.wait("vector", pe, mmn)
                for chn, v in vstage_free[t4]:
                    b.wait("vector", chn, v)
                b.op("vector", lambda e, bank=bank, t4=t4, psy=psy: e.tensor_copy(
                    out=vstage[:, t4 * 512:(t4 + 1) * 512],
                    in_=psy(bank)), inc=dve)
                psy_free[bank] = dve.n
                evn = dve.n
                b.wait("sync", dve, evn)
                b.wait("sync", pe, prev_pe)
                vf = []
                for tt in range(4 * t4, 4 * t4 + 4):
                    lo = (tt - 4 * t4) * 128 + t4 * 512
                    b.dma("sync", lambda e, bufA=bufA, tt=tt, lo=lo: e.dma_start(
                        out=vb[bufA][:, tt * 128:tt * 128 + 64],
                        in_=vstage[0:64, lo:lo + 128], transpose=True), vqA)
                    b.dma("sync", lambda e, bufB=bufB, tt=tt, lo=lo: e.dma_start(
                        out=vb[bufB][:, tt * 128:tt * 128 + 64],
                        in_=vstage[64:128, lo:lo + 128], transpose=True), vqB)
                    vb_rd[bufA][tt] = [(vqA, vqA.n)]
                    vb_rd[bufB][tt] = [(vqB, vqB.n)]
                    vf.extend([(vqA, vqA.n), (vqB, vqB.n)])
                vstage_free[t4] = vf

        build_sched = []
        init_settle = []

        with ExitStack() as pctx:
            xn_tok = pctx.enter_context(nc.sbuf_tensor("xn_tok", [128, 2048], BF16))
            sq_scr = pctx.enter_context(nc.sbuf_tensor("sq_scr", [128, 1024], F32))
            stats = pctx.enter_context(nc.sbuf_tensor("stats", [128, 32], F32))
            ptr = pctx.enter_context(nc.psum_tensor("ptr", [128, 2048], BF16))
            wdq_b = pctx.enter_context(nc.sbuf_tensor("wdq_b", [128, 2048], BF16))
            wdqT = pctx.enter_context(nc.sbuf_tensor("wdqT", [128, 32768], BF16))
            wu_mt = pctx.enter_context(nc.sbuf_tensor("wu_mt", [128, 12288], BF16))
            weff_sb = pctx.enter_context(nc.sbuf_tensor("weff_sb", [128, 12288], BF16))
            wdkv_b = pctx.enter_context(nc.sbuf_tensor("wdkv_b", [128, 2048], BF16))
            wkr_b = pctx.enter_context(nc.sbuf_tensor("wkr_b", [128, 256], BF16))
            stage = pctx.enter_context(nc.sbuf_tensor("stage", [128, 1024], BF16))
            ps_main = pctx.enter_context(nc.psum_tensor("ps_main", [128, 2048], F32))
            ps_init = pctx.enter_context(nc.psum_tensor("ps_init", [128, 1024], F32))

            def psy_p(bank):
                return ps_init[:, bank * 512:(bank + 1) * 512]
            psy_free_p = [0, 0]
            cast_slack = [0, 0]
            wdkv_reuse = [0]
            wkr_reuse = [0]

            pp_free = [(dve, 0)] * 4
            x_free = [0, 0, 0, 0]
            x_free_act = [0, 0, 0, 0]
            xn_free_pe = [0, 0]
            stats_free = [0, 0, 0, 0]
            ptr_free = [0, 0]
            sqscr_free = [0, 0]
            stage_free = [0, 0]
            cq_slot = [0]
            stg = [0]
            wdq_use = [0, 0]
            wuq_use = [0, 0]

            def pslot():
                s = cq_slot[0] % 4
                cq_slot[0] += 1
                return s

            def stage_slot():
                return 0

            def issue_xload_p(gt):
                par = gt % 4
                b.wait("sync", dve, x_free[par])
                b.wait("sync", act, x_free_act[par])
                b.dma("sync", lambda e, gt=gt, par=par: e.dma_start(
                    out=x_sb[:, par * 1024:(par + 1) * 1024],
                    in_=x_hbm[gt * 128:(gt + 1) * 128, :]), xq)
                xld_at[gt] = xq.n

            def issue_xdummy_p():
                b.wait("sync", dve, x_free[0])
                b.wait("sync", act, x_free_act[0])
                b.dma("sync", lambda e: e.dma_start(
                    out=x_sb[:, 0:1024], in_=x_hbm[0:128, :]), xq)
                xld_at[16] = xq.n

            xnT_rth = {0: [None, None], 1: [None, None]}
            xnT_rdy = {}
            st1 = {}

            if True:
                def ln_stage1(tt, tc):
                    gt = tc * 8 + tt
                    par = tt % 4
                    so = par * 8
                    nxt = gt + 3
                    if nxt <= 15:
                        issue_xload_p(nxt)
                    elif nxt == 16:
                        issue_xdummy_p()
                    ld = xld_at.get(gt + 2)
                    if ld is None:
                        ld = xld_at[gt + 1]
                    b.wait("vector", sq, const_sq)
                    b.wait("scalar", sq, const_sq)
                    b.wait("vector", xq, ld)
                    b.wait("vector", dve, stats_free[par])
                    b.op("vector", lambda e, par=par, so=so: e.reduce_sum(
                        out=stats[:, so:so + 1],
                        in_=x_sb[:, par * 1024:(par + 1) * 1024], axis=AX.X), inc=dve)
                    s1_n = dve.n
                    b.wait("scalar", xq, ld)
                    b.wait("scalar", dve, sqscr_free[0])
                    b.op("scalar", lambda e, par=par: e.activation(
                        out=sq_scr[:],
                        in_=x_sb[:, par * 1024:(par + 1) * 1024],
                        func=AF.Square), inc=act)
                    x_free_act[par] = act.n
                    sq_n = act.n
                    b.wait("vector", act, sq_n)
                    b.op("vector", lambda e, so=so: e.reduce_sum(
                        out=stats[:, so + 1:so + 2],
                        in_=sq_scr[:],
                        axis=AX.X), inc=dve)
                    sqscr_free[0] = dve.n
                    ss_n = dve.n
                    b.wait("scalar", dve, s1_n)
                    b.op("scalar", lambda e, so=so: e.activation(
                        out=stats[:, so + 2:so + 3], in_=stats[:, so:so + 1],
                        func=AF.Copy, scale=-1.0 / 1024.0), inc=act)
                    st1[tt] = (par, so, ss_n, act.n)

                def ln_stage2(tt, tc):
                    par, so, ss_n, nm_n = st1.pop(tt)
                    xpar = tt % 2
                    b.wait("vector", act, nm_n)
                    b.op("vector", lambda e, so=so: e.tensor_scalar(
                        out=stats[:, so + 4:so + 5], in0=stats[:, so + 2:so + 3],
                        scalar1=stats[:, so + 2:so + 3], scalar2=1e-5,
                        op0=ALU.mult, op1=ALU.subtract), inc=dve)
                    m2_n = dve.n
                    b.wait("vector", dve, max(ss_n, m2_n))
                    b.op("vector", lambda e, so=so: e.tensor_scalar(
                        out=stats[:, so + 5:so + 6], in0=stats[:, so + 1:so + 2],
                        scalar1=1.0 / 1024.0, scalar2=stats[:, so + 4:so + 5],
                        op0=ALU.mult, op1=ALU.subtract), inc=dve)
                    var_n = dve.n
                    b.wait("scalar", dve, var_n)
                    b.op("scalar", lambda e, so=so: e.activation(
                        out=stats[:, so + 6:so + 7], in_=stats[:, so + 5:so + 6],
                        func=AF.Ln), inc=act)
                    b.wait("scalar", act, act.n)
                    b.op("scalar", lambda e, so=so: e.activation(
                        out=stats[:, so + 3:so + 4], in_=stats[:, so + 6:so + 7],
                        func=AF.Exp, scale=-0.5), inc=act)
                    rstd_n = act.n
                    b.wait("vector", act, rstd_n)
                    b.wait("vector", pe, xn_free_pe[xpar])
                    b.op("vector", lambda e, par=par, so=so, xpar=xpar: e.tensor_scalar(
                        out=xn_tok[:, xpar * 1024:(xpar + 1) * 1024],
                        in0=x_sb[:, par * 1024:(par + 1) * 1024],
                        scalar1=stats[:, so + 2:so + 3], scalar2=stats[:, so + 3:so + 4],
                        op0=ALU.add, op1=ALU.mult), inc=dve)
                    x_free[par] = dve.n
                    xn_n = dve.n
                    stats_free[par] = dve.n
                    b.wait("tensor", sq, const_sq)
                    b.wait("tensor", dve, xn_n)
                    for c in range(8):
                        ps = c % 2
                        b.wait("tensor", dve, ptr_free[ps])
                        b.op("tensor", lambda e, c=c, ps=ps, xpar=xpar: e.transpose(
                            ptr[:, ps * 1024:ps * 1024 + 128],
                            xn_tok[:, xpar * 1024 + c * 128:xpar * 1024 + c * 128 + 128],
                            ident[:]), inc=pe)
                        trn = pe.n
                        b.wait("vector", pe, trn)
                        b.op("vector", lambda e, c=c, ps=ps, tc=tc, tt=tt: e.tensor_copy(
                            out=xnT[:, c * 2048 + tc * 1024 + tt * 128:
                                    c * 2048 + tc * 1024 + tt * 128 + 128],
                            in_=ptr[:, ps * 1024:ps * 1024 + 128]), inc=dve)
                        ptr_free[ps] = dve.n
                    xn_free_pe[xpar] = pe.n
                    if tt == 3:
                        xnT_rth[tc][0] = dve.n
                    if tt == 7:
                        xnT_rth[tc][1] = dve.n
                        xnT_rdy[tc] = dve.n

                def ln_piece(k, tc):
                    if k >= 1:
                        ln_stage2(k - 1, tc)
                    if k < 8:
                        ln_stage1(k, tc)

            def emit_ckv_kr(tc):
                if tc == 1:
                    b.wait("gpsimd", pe, wdkv_reuse[0])
                b.dma("gpsimd", lambda e: e.dma_start(
                    out=wdkv_b.rearrange("p (c m) -> p c m", m=256),
                    in_=w_dkv.rearrange("(c p) m -> p c m", p=128)), wq)
                ld = wq.n
                for m in range(2):
                    for th in range(2):
                        s = pslot()
                        b.wait("tensor", wq, ld)
                        b.wait("tensor", dve, xnT_rdy[tc])
                        b.wait("tensor", *pp_free[s])
                        for c in range(8):
                            b.op("tensor", lambda e, s=s, c=c, m=m, tc=tc, th=th: e.matmul(
                                ps_main[:, s * 512:(s + 1) * 512],
                                wdkv_b[:, c * 256 + m * 128:c * 256 + m * 128 + 128],
                                xnT[:, c * 2048 + tc * 1024 + th * 512:
                                    c * 2048 + tc * 1024 + th * 512 + 512],
                                start=(c == 0), stop=(c == 7)),
                                inc=(pe if c == 7 else None))
                        mmn = pe.n
                        b.wait("vector", pe, mmn)
                        b.op("vector", lambda e, s=s, m=m, tc=tc, th=th: e.tensor_copy(
                            out=ckvT[:, m * 2048 + tc * 1024 + th * 512:
                                     m * 2048 + tc * 1024 + th * 512 + 512],
                            in_=ps_main[:, s * 512:(s + 1) * 512]), inc=dve)
                        pp_free[s] = (dve, dve.n)
                wdkv_reuse[0] = pe.n
                ckv_rdy[0] = dve.n

                if tc == 1:
                    b.wait("gpsimd", pe, wkr_reuse[0])
                b.dma("gpsimd", lambda e: e.dma_start(
                    out=wkr_b.rearrange("p (c m) -> p c m", m=32),
                    in_=w_kr.rearrange("(c p) m -> p c m", p=128)), wq)
                ld = wq.n
                for th in range(2):
                    s = pslot()
                    b.wait("tensor", wq, ld)
                    b.wait("tensor", *pp_free[s])
                    for c in range(8):
                        b.op("tensor", lambda e, s=s, c=c, tc=tc, th=th: e.matmul(
                            ps_main[0:32, s * 512:(s + 1) * 512],
                            wkr_b[:, c * 32:c * 32 + 32],
                            xnT[:, c * 2048 + tc * 1024 + th * 512:
                                c * 2048 + tc * 1024 + th * 512 + 512],
                            start=(c == 0), stop=(c == 7)),
                            inc=(pe if c == 7 else None))
                    mmn = pe.n
                    off = tc * 1024 + th * 512
                    b.wait("vector", pe, mmn)
                    b.wait("vector", sq, const_sq)
                    b.op("vector", lambda e, s=s, off=off: e.tensor_tensor(
                        out=krT[0:32, off:off + 512],
                        in0=ps_main[0:32, s * 512:(s + 1) * 512],
                        in1=rope[0:32, off:off + 512], op=ALU.mult), inc=dve)
                    pp_free[s] = (dve, dve.n)
                wkr_reuse[0] = pe.n

            for i3 in range(2):
                b.op("gpsimd", lambda e, i3=i3: e.memset(vb[i3][:], 1.0),
                     inc=pool)
            vb_init_n = pool.n

            wdq_ld = [0] * 32
            wu_ld = {}
            wu_war = [0] * 32
            wdq_war = [0, 0]
            wdqT_rd = [0] * 32
            weff_mm = [[0] * 4 for _ in range(2)]

            def emit_wu_load(mt, d):
                b.wait("gpsimd", pe, wu_war[d])
                lo, hi = mt * 384, mt * 384 + 384
                spans = []
                if lo < 1024:
                    spans.append((w_uq, lo, min(hi, 1024), 0))
                if hi > 1024:
                    spans.append((w_qr, max(lo, 1024) - 1024, hi - 1024,
                                  max(lo, 1024) - lo))
                for srcw, c0s, c1s, off in spans:
                    b.dma("gpsimd", lambda e, srcw=srcw, c0s=c0s, c1s=c1s, off=off, d=d: e.dma_start(
                        out=wu_mt[:, d * 384 + off:d * 384 + off + (c1s - c0s)],
                        in_=srcw[d * 128:(d + 1) * 128, c0s:c1s]), wq)
                wu_ld[(mt, d)] = wq.n
                if d == 31:
                    for _ in range(2):
                        b.dma("gpsimd", lambda e: e.dma_start(
                            out=onesbf[:], in_=onesbf_h), wq)

            def emit_weff_mms(mt, cbg, d):
                b.wait("tensor", wq, wu_ld[(mt, d)] + 32)
                b.wait("tensor", dve, wdqT_rd[d])
                for cb4 in range(4):
                    cb = cbg * 4 + cb4
                    if d == 0:
                        b.wait("tensor", *pp_free[cb4])
                    b.op("tensor", lambda e, cb=cb, cb4=cb4, d=d: e.matmul(
                        ps_main[:, cb4 * 512:cb4 * 512 + 384],
                        wdqT[:, d * 1024 + cb * 128:d * 1024 + cb * 128 + 128],
                        wu_mt[:, d * 384:(d + 1) * 384],
                        start=(d == 0), stop=(d == 31)),
                        inc=(pe if (d == 31 or (cbg == 1 and cb4 == 3)) else None))
                    if d == 31:
                        weff_mm[cbg][cb4] = pe.n

            def emit_weff_evac(mt, cbg):
                for cb4 in range(4):
                    cb = cbg * 4 + cb4
                    b.wait("vector", pe, weff_mm[cbg][cb4])
                    b.op("vector", lambda e, mt=mt, cb=cb, cb4=cb4: e.tensor_copy(
                        out=weff_sb[:, cb * 1536 + mt * 384:cb * 1536 + (mt + 1) * 384],
                        in_=ps_main[:, cb4 * 512:cb4 * 512 + 384]), inc=dve)
                    pp_free[cb4] = (dve, dve.n)

            for tc9 in range(2):
                for k9 in range(9):
                    build_sched.append(lambda k9=k9, tc9=tc9: ln_piece(k9, tc9))

            for i in range(32):
                par = i % 2
                b.wait("gpsimd", pe, wdq_war[par])
                b.dma("gpsimd", lambda e, i=i, par=par: e.dma_start(
                    out=wdq_b[:, par * 1024:(par + 1) * 1024]
                        .rearrange("p (c d) -> p c d", d=128),
                    in_=w_dq[:, i * 128:(i + 1) * 128]
                        .rearrange("(c p) d -> p c d", p=128)), wq)
                wdq_ld[i] = wq.n
                emit_wu_load(0, i)
                ps = i % 2
                b.wait("tensor", wq, wu_ld[(0, i)] + (16 if i == 31 else 32))
                b.wait("tensor", dve, ptr_free[ps])
                b.wait("tensor", sq, ident_sq)
                for cb in range(8):
                    b.op("tensor", lambda e, cb=cb, ps=ps, par=par: e.transpose(
                        ptr[:, ps * 1024 + cb * 128:ps * 1024 + cb * 128 + 128],
                        wdq_b[:, par * 1024 + cb * 128:par * 1024 + cb * 128 + 128],
                        ident[:]), inc=(pe if cb == 7 else None))
                wdq_war[par] = pe.n
                b.wait("vector", pe, pe.n)
                b.op("vector", lambda e, i=i, ps=ps: e.tensor_copy(
                    out=wdqT[:, i * 1024:(i + 1) * 1024],
                    in_=ptr[:, ps * 1024:(ps + 1) * 1024]), inc=dve)
                wdqT_rd[i] = dve.n
                ptr_free[ps] = dve.n
                emit_weff_mms(0, 0, i)
                if build_sched:
                    build_sched.pop(0)()
            emit_weff_evac(0, 0)
            emit_ckv_kr(0)

            for mt in range(4):
                for cbg in range(2):
                    if mt == 0 and cbg == 0:
                        continue
                    for d in range(32):
                        emit_weff_mms(mt, cbg, d)
                        if cbg == 1:
                            wu_war[d] = pe.n
                            if mt < 3:
                                emit_wu_load(mt + 1, d)
                    emit_weff_evac(mt, cbg)
            weff_rdy = dve.n

            emit_ckv_kr(1)

            b.wait("scalar", dve, max(x_free))
            b.wait("scalar", xq, xld_at[16])
            for src, lo in ((w_uk, 0), (w_uk, 1), (w_uv, 2), (w_uv, 3)):
                b.dma("scalar", lambda e, src=src, lo=lo: e.dma_start(
                    out=x_sb[:, lo * 1024:(lo + 1) * 1024],
                    in_=src[(lo % 2) * 128:(lo % 2) * 128 + 128, :]), aq)
                if lo == 2:
                    cast_slack[0] = aq.n
            b.dma("scalar", lambda e: e.dma_start(out=onesbf[:], in_=onesbf_h), aq)
            cast_slack[1] = aq.n
            b.wait("scalar", pe, pe.n)
            b.wait("scalar", aq, cast_slack[1])
            for hh in range(2):
                b.op("scalar", lambda e, hh=hh: e.activation(
                    out=wdqT[:, hh * 1024:(hh + 1) * 1024],
                    in_=x_sb[:, 2048 + hh * 1024:2048 + (hh + 1) * 1024],
                    func=AF.Copy), inc=act)
            uk_ld[0] = act.n
            build_sched.append(lambda: b.wait("sync", pool, vb_init_n))
            for kk in range(4):
                build_sched.append(lambda kk=kk: emit_build_tile(
                    0, kk, 0, psy_p, psy_free_p,
                    wdqT[:, 0:2048], wdqT[:, 0:2048],
                    (act, uk_ld[0])))

            for tc in range(2):
                boff = tc * 1024
                for j in range(12):
                    for th in range(2):
                        s = pslot()
                        b.wait("tensor", dve, weff_rdy)
                        b.wait("tensor", *pp_free[s])
                        b.wait("tensor", dve, xnT_rdy[tc])
                        for c in range(8):
                            b.op("tensor", lambda e, s=s, c=c, j=j, tc=tc, th=th: e.matmul(
                                ps_main[:, s * 512:(s + 1) * 512],
                                weff_sb[:, c * 1536 + j * 128:c * 1536 + j * 128 + 128],
                                xnT[:, c * 2048 + tc * 1024 + th * 512:
                                    c * 2048 + tc * 1024 + th * 512 + 512],
                                start=(c == 0), stop=(c == 7)),
                                inc=(pe if c == 7 else None))
                        mmn = pe.n
                        off = tc * 1024 + th * 512
                        if j < 8:
                            b.wait("scalar", pe, mmn)
                            b.wait("scalar", aq, stage_free[0])
                            b.op("scalar", lambda e, s=s, th=th: e.activation(
                                out=stage[:, th * 512:th * 512 + 512],
                                in_=ps_main[:, s * 512:(s + 1) * 512],
                                func=AF.Copy), inc=act)
                            pp_free[s] = (act, act.n)
                        else:
                            b.wait("vector", pe, mmn)
                            b.wait("vector", aq, stage_free[0])
                            b.wait("vector", sq, const_sq)
                            b.op("vector", lambda e, s=s, th=th, off=off: e.tensor_tensor(
                                out=stage[:, th * 512:th * 512 + 512],
                                in0=ps_main[:, s * 512:(s + 1) * 512],
                                in1=rope[:, off:off + 512],
                                op=ALU.mult), inc=dve)
                            pp_free[s] = (dve, dve.n)
                    if j < 8:
                        b.wait("scalar", act, act.n)
                        b.dma("scalar", lambda e, j=j, boff=boff: e.dma_start(
                            out=qt_hbm[2 * j:2 * j + 2, 0:64, boff:boff + 1024],
                            in_=stage[:, 0:1024]), aq)
                    else:
                        cpn = dve.n
                        jj = j - 8
                        b.wait("scalar", dve, cpn)
                        b.dma("scalar", lambda e, jj=jj, boff=boff: e.dma_start(
                            out=qt_hbm[4 * jj:4 * jj + 4, 64:96, boff:boff + 1024],
                            in_=stage[:, 0:1024]), aq)
                    stage_free[0] = aq.n
                    if build_sched:
                        build_sched.pop(0)()

            assert not build_sched
            init_settle.extend([(vqA, vqA.n), (vqB, vqB.n), (dve, dve.n)])

            if debug:
                b.wait("sync", dve, dve.n)
                b.wait("sync", act, act.n)
                b.dma("sync", lambda e: e.dma_start(out=dbg["weff"], in_=weff_sb[:]), sq)
                b.dma("sync", lambda e: e.dma_start(out=dbg["xnT"], in_=xnT[:]), sq)
                b.dma("sync", lambda e: e.dma_start(out=dbg["ckvT"], in_=ckvT[:]), sq)
                b.dma("sync", lambda e: e.dma_start(out=dbg["krT"], in_=krT[:]), sq)
            b.barrier(ALLCH)
            flush()

        if phases < 2:
            return nc

        yT = ctx.enter_context(nc.sbuf_tensor("yT", [128, 16384], BF16))
        wo_sb = ctx.enter_context(nc.sbuf_tensor("wo_sb", [128, 8192], BF16))
        cbt_sb = ctx.enter_context(nc.sbuf_tensor("cbt_sb", [128, 8192], BF16))
        with ExitStack() as actx:
            wuk_sb = actx.enter_context(nc.sbuf_tensor("wuk_sb", [128, 2048], BF16))
            wuv_sb = actx.enter_context(nc.sbuf_tensor("wuv_sb", [128, 2048], BF16))
            ktb[0] = actx.enter_context(nc.sbuf_tensor("ktb0", [96, 2048], BF16))
            ktb[1] = actx.enter_context(nc.sbuf_tensor("ktb1", [96, 2048], BF16))
            ktb[2] = actx.enter_context(nc.sbuf_tensor("ktb2", [96, 2048], BF16))
            vb[2] = actx.enter_context(nc.sbuf_tensor("vb2", [128, 2048], BF16))
            qtb = [actx.enter_context(nc.sbuf_tensor(f"qtb{i}", [96, 2048], BF16))
                   for i in range(2)]
            masks = actx.enter_context(nc.sbuf_tensor("masks", [128, 2560], BF16))
            pt = actx.enter_context(nc.sbuf_tensor("pt", [128, 2048], BF16))
            ln_tmp = actx.enter_context(nc.sbuf_tensor("ln_tmp", [64, 1024], F32))
            rb = actx.enter_context(nc.sbuf_tensor("rb", [64, 1024], F32))
            pss_all = actx.enter_context(nc.psum_tensor("pss_all", [128, 2048], F32))

            def pss_v(s, bank):
                return pss_all[:, (2 * bank + s) * 512:(2 * bank + s + 1) * 512]
            ps_yp = [actx.enter_context(nc.psum_tensor(f"ps_yp{q}", [128, 1024], F32))
                     for q in range(2)]

            def psy(g):
                return ps_yp[g // 2][:, (g % 2) * 512:(g % 2 + 1) * 512]

            b.dma("gpsimd", lambda e: e.dma_start(
                out=wuv_sb.rearrange("p (c m) -> p c m", m=1024),
                in_=w_uv.rearrange("(c p) m -> p c m", p=128)), wq)
            uv_ld_a = wq.n
            b.dma("gpsimd", lambda e: e.dma_start(
                out=wo_sb.rearrange("p (f m) -> p f m", m=1024),
                in_=w_o.rearrange("(f p) m -> p f m", p=128)), wq)
            b.dma("gpsimd", lambda e: e.dma_start(
                out=cbt_sb.rearrange("p (c m) -> p c m", m=1024),
                in_=cbt.rearrange("(c p) m -> p c m", p=128)), wq)
            fw_ld = wq.n

            for hh in range(2):
                b.op("scalar", lambda e, hh=hh: e.activation(
                    out=wuk_sb[:, hh * 1024:(hh + 1) * 1024],
                    in_=x_sb[:, hh * 1024:(hh + 1) * 1024],
                    func=AF.Copy), inc=act)
            uk_cast = act.n

            b.op("gpsimd", lambda e: e.memset(vb[2][:], 1.0), inc=pool)
            amem = pool.n
            for i3 in range(3):
                b.op("vector", lambda e, i3=i3: e.tensor_copy(
                    out=ktb[i3][64:96, :], in_=krT[:]), inc=dve)
            ainit = dve.n

            b.dma("sync", lambda e: e.dma_start(out=masks[:], in_=masks_h), sq)
            masks_ld = sq.n
            b.wait("tensor", sq, masks_ld)
            b.dma("scalar", lambda e: e.dma_start(out=qtb[0][:], in_=qt_hbm[0]), pq)
            qt_ld = [pq.n, 0]
            b.dma("scalar", lambda e: e.dma_start(out=qtb[1][:], in_=qt_hbm[1]), pq)
            qt_ld[1] = pq.n

            war_head = [0] * 17
            psy_free = [0, 0, 0, 0]
            exp_bank = [[0, 0], [0, 0]]
            rb_free = [0]

            STREAM = [[(0, j) for j in range(4)] + [(3, j) for j in range(16)],
                      [(1, j) for j in range(8)] + [(2, j) for j in range(12)]]

            pt_pv = [[0, 0], [0, 0]]
            fin_ready = [0, 0]

            def emit_scores(h, s, g, j, par, bank):
                boundary = (g == j // 4)
                c0 = max(0, j * 128 - 512 * g)
                b.wait("tensor", dve, kt_rd[par][j // 4])
                b.op("tensor", lambda e, s=s, g=g, j=j, par=par, bank=bank, c0=c0: e.matmul(
                    pss_v(s, bank)[:, c0:512],
                    ktb[par][0:96, j * 128:j * 128 + 128],
                    qtb[h % 2][0:96, g * 512 + c0:(g + 1) * 512],
                    start=True, stop=not boundary),
                    inc=(None if boundary else pe))
                if boundary:
                    r = j % 4
                    b.op("tensor", lambda e, s=s, bank=bank, r=r, c0=c0: e.matmul(
                        pss_v(s, bank)[:, c0:c0 + 128],
                        ident[:], masks[:, r * 512 + c0:r * 512 + c0 + 128],
                        start=False, stop=True), inc=pe)
                return c0

            def emit_exp2(bank, c0A):
                smm = pe.n
                b.wait("scalar", pe, smm)
                b.wait("scalar", pe, max(pt_pv[0][bank], pt_pv[1][bank]))
                lo = 2 * bank * 512 + c0A
                hi = (2 * bank + 2) * 512
                b.op("scalar", lambda e, lo=lo, hi=hi: e.activation(
                    out=pt[:, lo:hi], in_=pss_all[:, lo:hi],
                    func=AF.Exp, scale=INV_SQRT), inc=act)
                exp_bank[0][bank] = exp_bank[1][bank] = act.n
                return act.n

            def emit_pv(h, s, g, j, par, hq, bank, expn):
                reg = (2 * bank + s) * 512
                c0 = max(0, j * 128 - 512 * g)
                for chn, v in vb_rd[par][j]:
                    b.wait("tensor", chn, v)
                if j == 0:
                    b.wait("tensor", dve, psy_free[g])
                b.op("tensor", lambda e, g=g, j=j, par=par, reg=reg, c0=c0: e.matmul(
                    psy(g)[:, c0:512],
                    vb[par][:, j * 128:(j + 1) * 128],
                    pt[:, reg + c0:reg + 512],
                    start=(j == 0), stop=(j == 4 * g + 3)), inc=pe)
                pt_pv[s][bank] = pe.n
                if j == 4 * g + 3 and (g % 2 == 1 or g == 2):
                    fin_ready[g // 2] += 1
                    if g // 2 == 0 or fin_ready[1] == 2:
                        emit_finish(h, g // 2, hq)

            finish_q = []

            def emit_finish(h, q, hq):
                psn = pe.n

                def p_ln(q=q, psn=psn):
                    b.wait("scalar", pe, psn)
                    b.wait("scalar", dve, rb_free[0])
                    b.op("scalar", lambda e, q=q: e.activation(
                        out=ln_tmp[:],
                        in_=ps_yp[q][64:128, :],
                        func=AF.Ln), inc=act)

                def p_exp():
                    b.wait("scalar", act, act.n)
                    b.op("scalar", lambda e: e.activation(
                        out=rb[:], in_=ln_tmp[:],
                        func=AF.Exp, scale=-1.0), inc=act)

                def p_mult(q=q, h=h, hq=hq):
                    b.wait("vector", act, act.n)
                    b.op("vector", lambda e, q=q, h=h, hq=hq: e.tensor_tensor(
                        out=yT[64 * (h % 2):64 * (h % 2) + 64,
                               hq + q * 1024:hq + (q + 1) * 1024],
                        in0=ps_yp[q][0:64, :], in1=rb[:], op=ALU.mult), inc=dve)
                    psy_free[2 * q] = psy_free[2 * q + 1] = dve.n
                    rb_free[0] = dve.n

                finish_q.extend([p_ln, p_exp, p_mult])

            for kk in range(4, 8):
                emit_build_tile(0, kk, 0, psy, psy_free,
                                wuk_sb[:], wuk_sb[:], (act, uk_cast))
            if debug:
                for chn, v in init_settle:
                    b.wait("sync", chn, v)
                b.dma("sync", lambda e: e.dma_start(out=dbg["ktb0"], in_=ktb[0][:]), sq)
                b.dma("sync", lambda e: e.dma_start(out=dbg["ktb1"], in_=ktb[1][:]), sq)
                b.dma("sync", lambda e: e.dma_start(out=dbg["vb0"], in_=vb[0][:]), sq)
                b.dma("sync", lambda e: e.dma_start(out=dbg["vb1"], in_=vb[1][:]), sq)
                for eng2 in ("tensor", "vector", "scalar"):
                    b.wait(eng2, sq, sq.n)

            for h in range(16):
                par = h % 3
                hq = (h // 2) * 2048
                b.wait("tensor", pq, qt_ld[h % 2])
                b.wait("tensor", dve, ainit)
                b.wait("tensor", pool, amem)
                if h < 2:
                    for chn, v in init_settle:
                        b.wait("tensor", chn, v)
                b.wait("vector", sq, const_sq)
                build_k = [0]
                fin_ready[0] = fin_ready[1] = 0
                pend = []
                for i in range(22):
                    for _ in range(2):
                        if finish_q:
                            finish_q.pop(0)()
                    if len(pend) == 2 or (i >= 20 and pend):
                        (pgA, pjA, pgB, pjB, pexp, pbank) = pend.pop(0)
                        b.wait("tensor", act, pexp)
                        emit_pv(h, 0, pgA, pjA, par, hq, pbank, pexp)
                        emit_pv(h, 1, pgB, pjB, par, hq, pbank, pexp)
                    if i < 20:
                        if i < 2:
                            b.wait("tensor", act, exp_bank[0][i % 2])
                        gA, jA = STREAM[0][i]
                        c0A = emit_scores(h, 0, gA, jA, par, i % 2)
                        gB, jB = STREAM[1][i]
                        emit_scores(h, 1, gB, jB, par, i % 2)
                        en = emit_exp2(i % 2, c0A)
                        pend.append((gA, jA, gB, jB, en, i % 2))
                    if i == 8 and h < 15:
                        npar2 = (h + 1) % 2
                        b.wait("scalar", pe, war_head[h - 1] if h >= 1 else 0)
                        b.dma("scalar", lambda e, h=h, npar2=npar2: e.dma_start(
                            out=qtb[npar2][:], in_=qt_hbm[h + 1]), pq)
                        qt_ld[npar2] = pq.n
                    if i >= 13 and (h % 2 == 1) and h < 15 and build_k[0] < 8:
                        kk2 = build_k[0]
                        if kk2 == 0:
                            b.wait("sync", pool, amem)
                        emit_build_tile((h + 1) // 2, kk2,
                                        war_head[h - 1] if h >= 1 else 0,
                                        psy, psy_free, wuk_sb[:], wuv_sb[:],
                                        (wq, uv_ld_a) if kk2 < 4 else
                                        (act, uk_cast))
                        build_k[0] += 1
                war_head[h] = pe.n
            while finish_q:
                finish_q.pop(0)()

            if debug:
                b.wait("sync", dve, dve.n)
                b.dma("sync", lambda e: e.dma_start(out=dbg["yT"], in_=yT[:]), sq)
            b.barrier(ALLCH)
            flush()

        if phases < 3:
            return nc

        with ExitStack() as fctx:
            scb_bc = fctx.enter_context(nc.sbuf_tensor("scb_bc", [128, 1024], F32))
            scb_row = fctx.enter_context(nc.sbuf_tensor("scb_row", [1, 1024], F32))
            bias_row = fctx.enter_context(nc.sbuf_tensor("bias_row", [1, 1024], F32))
            rr1 = fctx.enter_context(nc.sbuf_tensor("rr1", [1, 1024], F32))
            rr2 = fctx.enter_context(nc.sbuf_tensor("rr2", [1, 1024], F32))
            browbf = fctx.enter_context(nc.sbuf_tensor("browbf", [1, 1024], BF16))
            t1 = fctx.enter_context(nc.sbuf_tensor("t1", [128, 2048], F32))
            g_sb = fctx.enter_context(nc.sbuf_tensor("g_sb", [128, 2048], F32))
            t3 = fctx.enter_context(nc.sbuf_tensor("t3", [128, 2048], F32))
            out_sb = fctx.enter_context(nc.sbuf_tensor("out_sb", [128, 2048], F32))
            pf = fctx.enter_context(nc.psum_tensor("pf", [128, 1024], F32))
            psf = fctx.enter_context(nc.psum_tensor("psf", [128, 1024], F32))
            pb = fctx.enter_context(nc.psum_tensor("pb", [128, 1024], F32))

            b.dma("sync", lambda e: e.dma_start(out=scb_row[:], in_=scb_h), sq)
            b.dma("sync", lambda e: e.dma_start(out=bias_row[:], in_=bias_h), sq)
            b.dma("sync", lambda e: e.dma_start(out=c127[:], in_=c127_h), sq)
            rows_ld = sq.n
            b.wait("tensor", sq, rows_ld)
            for hf in range(2):
                b.op("tensor", lambda e, hf=hf: e.matmul(
                    pb[:, hf * 512:(hf + 1) * 512], c127[:],
                    scb_row[0:1, hf * 512:(hf + 1) * 512], start=True, stop=True), inc=pe)
            bc1 = pe.n
            b.wait("vector", pe, bc1)
            b.op("vector", lambda e: e.tensor_copy(out=scb_bc[:], in_=pb[:]), inc=dve)
            bcast_n = dve.n
            b.wait("scalar", sq, rows_ld)
            b.op("scalar", lambda e: e.activation(
                out=rr1[:], in_=scb_row[:], func=AF.Ln, scale=1.0 / 127.0), inc=act)
            b.wait("scalar", act, act.n)
            b.op("scalar", lambda e: e.activation(
                out=rr2[:], in_=rr1[:], func=AF.Exp, scale=-1.0), inc=act)
            rr_n = act.n
            b.wait("vector", act, rr_n)
            b.op("vector", lambda e: e.tensor_tensor(
                out=browbf[:], in0=bias_row[:], in1=rr2[:], op=ALU.mult), inc=dve)
            brow_n = dve.n

            b.wait("tensor", wq, fw_ld)

            pf_free = [0, 0]
            psf_free = [0, 0]
            t1_free = [[0, 0], [0, 0]]
            g_free = [[0, 0], [0, 0]]
            t3_free = [[0, 0], [0, 0]]
            out_free = [0, 0]
            out_free4 = [0, 0, 0, 0]

            xld_at = [0] * (NT + 2)

            def issue_xload(i):
                xpar = i % 4
                b.wait("sync", st, out_free4[xpar])
                b.dma("sync", lambda e, i=i, xpar=xpar: e.dma_start(
                    out=x_sb[:, xpar * 1024:(xpar + 1) * 1024],
                    in_=x_hbm[i * 128:(i + 1) * 128, :]), xq)
                xld_at[i] = xq.n

            issue_xload(0)
            issue_xload(1)
            for i in range(NT):
                par = i % 2
                if i + 2 < NT:
                    issue_xload(i + 2)
                xld = xld_at[i]
                for nh in range(2):
                    b.wait("tensor", act, t1_free[par][nh])
                    b.wait("tensor", dve, pf_free[nh])
                    for c in range(8):
                        b.op("tensor", lambda e, c=c, i=i, nh=nh: e.matmul(
                            pf[:, nh * 512:(nh + 1) * 512],
                            xnT[:, c * 2048 + i * 128:c * 2048 + i * 128 + 128],
                            cbt_sb[:, c * 1024 + nh * 512:c * 1024 + nh * 512 + 512],
                            start=(c == 0), stop=False))
                    b.wait("tensor", dve, brow_n)
                    b.op("tensor", lambda e, nh=nh: e.matmul(
                        pf[:, nh * 512:(nh + 1) * 512], onesbf[:],
                        browbf[0:1, nh * 512:(nh + 1) * 512],
                        start=False, stop=True), inc=pe)
                    ffn_n = pe.n
                    b.wait("vector", pe, ffn_n)
                    b.wait("vector", dve, bcast_n)
                    b.wait("vector", act, t1_free[par][nh])
                    b.op("vector", lambda e, nh=nh, par=par: e.tensor_tensor(
                        out=t1[:, par * 1024 + nh * 512:par * 1024 + nh * 512 + 512],
                        in0=pf[:, nh * 512:(nh + 1) * 512],
                        in1=scb_bc[:, nh * 512:(nh + 1) * 512], op=ALU.mult), inc=dve)
                    pf_free[nh] = dve.n
                    t1n = dve.n
                    b.wait("scalar", dve, t1n)
                    b.wait("scalar", dve, g_free[par][nh])
                    b.op("scalar", lambda e, nh=nh, par=par: e.activation(
                        out=g_sb[:, par * 1024 + nh * 512:par * 1024 + nh * 512 + 512],
                        in_=t1[:, par * 1024 + nh * 512:par * 1024 + nh * 512 + 512],
                        func=AF.Silu), inc=act)
                    t1_free[par][nh] = act.n
                    gn = act.n
                    b.wait("tensor", dve, psf_free[nh])
                    for f in range(8):
                        b.op("tensor", lambda e, f=f, i=i, nh=nh: e.matmul(
                            psf[:, nh * 512:(nh + 1) * 512],
                            yT[:, f * 2048 + i * 128:f * 2048 + i * 128 + 128],
                            wo_sb[:, f * 1024 + nh * 512:f * 1024 + nh * 512 + 512],
                            start=(f == 0), stop=(f == 7)),
                            inc=(pe if f == 7 else None))
                    wo_n = pe.n
                    b.wait("vector", pe, wo_n)
                    b.wait("vector", act, gn)
                    b.wait("vector", pool, t3_free[par][nh])
                    b.op("vector", lambda e, nh=nh, par=par: e.tensor_tensor(
                        out=t3[:, par * 1024 + nh * 512:par * 1024 + nh * 512 + 512],
                        in0=psf[:, nh * 512:(nh + 1) * 512],
                        in1=g_sb[:, par * 1024 + nh * 512:par * 1024 + nh * 512 + 512],
                        op=ALU.mult), inc=dve)
                    psf_free[nh] = dve.n
                    g_free[par][nh] = dve.n
                    t3n = dve.n
                    b.wait("gpsimd", dve, t3n)
                    b.wait("gpsimd", xq, xld)
                    b.op("gpsimd", lambda e, nh=nh, par=par, xpar=i % 4: e.tensor_tensor(
                        out=out_sb[:, par * 1024 + nh * 512:par * 1024 + nh * 512 + 512],
                        in0=t3[:, par * 1024 + nh * 512:par * 1024 + nh * 512 + 512],
                        in1=x_sb[:, xpar * 1024 + nh * 512:xpar * 1024 + nh * 512 + 512],
                        op=ALU.add), inc=pool)
                    t3_free[par][nh] = pool.n
                outn = pool.n
                b.wait("sync", pool, outn)
                b.dma("sync", lambda e, i=i, par=par: e.dma_start(
                    out=y_hbm[i * 128:(i + 1) * 128, :],
                    in_=out_sb[:, par * 1024:(par + 1) * 1024]), st)
                out_free[par] = st.n
                out_free4[i % 4] = st.n

            b.wait("sync", st, st.n)
            flush()

    return nc


def _rope_scale_np():
    t = np.arange(T, dtype=np.float32)
    j = np.arange(0, DHR, 2, dtype=np.float32) / np.float32(DHR)
    inv = (np.float32(1.0) / np.power(np.float32(10000.0), j)).astype(np.float32)
    freqs = (t[:, None] * inv[None, :]).astype(np.float32)
    emb = np.concatenate([freqs, freqs], axis=-1)
    scale = (np.cos(emb) + np.sin(emb)).astype(np.float32)
    rep = np.ascontiguousarray(np.tile(scale.T, (4, 1)).astype(np.float32))
    return rep


def _masks_np():
    kk = np.arange(128)[:, None]
    qp = np.arange(512)[None, :]
    out = np.zeros((128, 2560), dtype=ml_dtypes.bfloat16)
    for r in range(4):
        out[:, r * 512:(r + 1) * 512] = np.where(
            qp < 128 * r + kk, np.float32(-1e5), np.float32(0)
        ).astype(ml_dtypes.bfloat16)
    return out


def make_inputs(inp):
    core_common = dict(
        W_DQ=np.ascontiguousarray(np.asarray(inp["L0_attn_W_DQ"], dtype=np.float32)),
        W_UQ=np.ascontiguousarray(np.asarray(inp["L0_attn_W_UQ"], dtype=np.float32)),
        W_DKV=np.ascontiguousarray(np.asarray(inp["L0_attn_W_DKV"], dtype=np.float32)),
        W_UK=np.ascontiguousarray(np.asarray(inp["L0_attn_W_UK"], dtype=np.float32)),
        W_UV=np.ascontiguousarray(np.asarray(inp["L0_attn_W_UV"], dtype=np.float32)),
        W_QR=np.ascontiguousarray(np.asarray(inp["L0_attn_W_QR"], dtype=np.float32)),
        W_KR=np.ascontiguousarray(np.asarray(inp["L0_attn_W_KR"], dtype=np.float32)),
        W_O=np.ascontiguousarray(np.asarray(inp["L0_attn_W_O"], dtype=np.float32)),
        CBT=np.ascontiguousarray(np.asarray(inp["L0_ffn_CB"], dtype=np.float32).T),
        SCB=np.ascontiguousarray(np.asarray(inp["L0_ffn_SCB"], dtype=np.float32).reshape(1, C)),
        BIAS=np.ascontiguousarray(np.asarray(inp["L0_ffn_bias"], dtype=np.float32).reshape(1, C)),
        ROPE=_rope_scale_np().astype(ml_dtypes.bfloat16),
        MASKS=_masks_np(),
        C127=np.full((1, 128), 1.0 / 127.0, dtype=np.float32),
        ONES=np.ones((1, 128), dtype=np.float32),
        ONESBF=np.ones((1, 128), dtype=ml_dtypes.bfloat16),
        IDENT=np.eye(128, dtype=ml_dtypes.bfloat16),
    )
    x = np.ascontiguousarray(np.asarray(inp["L0_attn_x"], dtype=np.float32))
    in_maps = []
    for bcore in range(8):
        m = dict(core_common)
        m["x"] = np.ascontiguousarray(x[bcore])
        in_maps.append(m)
    return in_maps


def kernel(**inputs):
    in_maps = make_inputs(inputs)
    nc = build_program()
    res = run_bass_kernel_spmd(nc, in_maps, core_ids=list(range(8)))
    return np.stack([r["y"] for r in res.results], axis=0)


# version 43
# speedup vs baseline: 1.0639x; 1.0639x over previous
import math
import numpy as np
import ml_dtypes
from contextlib import ExitStack

import concourse.bass as bass
import concourse.mybir as mybir
from concourse.bass_utils import run_bass_kernel_spmd

F32 = mybir.dt.float32
BF16 = mybir.dt.bfloat16
AF = mybir.ActivationFunctionType
ALU = mybir.AluOpType
AX = mybir.AxisListType

T = 2048
C = 1024
H = 16
HS = 64
DHR = 32
DQK = HS + DHR
DC = 4096
DKV = 256
NT = T // 128
INV_SQRT = 1.0 / math.sqrt(DQK)

ENGINES = ("tensor", "vector", "scalar", "gpsimd", "sync")


class Chan:
    def __init__(self, sem):
        self.sem = sem
        self.n = 0


class Builder:

    def __init__(self):
        self.plan = {e: [] for e in ENGINES}
        self.lastw = {e: {} for e in ENGINES}

    def wait(self, eng, ch, val=None):
        v = ch.n if val is None else val
        if v <= 0:
            return
        if self.lastw[eng].get(id(ch), -1) >= v:
            return
        self.lastw[eng][id(ch)] = v
        self.plan[eng].append(lambda e, s=ch.sem, vv=v: e.wait_ge(s, vv))

    def op(self, eng, fn, inc=None, k=1):
        if inc is not None:
            inc.n += k
            self.plan[eng].append(
                lambda e, f=fn, s=inc.sem, kk=k: f(e).then_inc(s, kk))
        else:
            self.plan[eng].append(fn)

    def dma(self, eng, fn, ch):
        self.op(eng, fn, inc=ch, k=16)

    def barrier(self, chans):
        snap = [(ch, ch.n) for ch in chans]
        for eng in ENGINES:
            for ch, v in snap:
                self.wait(eng, ch, v)


def build_program(debug=False, phases=3):
    nc = bass.Bass("TRN2", target_bir_lowering=False)

    x_hbm = nc.dram_tensor("x", [T, C], F32, kind="ExternalInput").ap()
    w_dq = nc.dram_tensor("W_DQ", [C, DC], F32, kind="ExternalInput").ap()
    w_uq = nc.dram_tensor("W_UQ", [DC, C], F32, kind="ExternalInput").ap()
    w_dkv = nc.dram_tensor("W_DKV", [C, DKV], F32, kind="ExternalInput").ap()
    w_uk = nc.dram_tensor("W_UK", [DKV, C], F32, kind="ExternalInput").ap()
    w_uv = nc.dram_tensor("W_UV", [DKV, C], F32, kind="ExternalInput").ap()
    w_qr = nc.dram_tensor("W_QR", [DC, H * DHR], F32, kind="ExternalInput").ap()
    w_kr = nc.dram_tensor("W_KR", [C, DHR], F32, kind="ExternalInput").ap()
    w_o = nc.dram_tensor("W_O", [C, C], F32, kind="ExternalInput").ap()
    cbt = nc.dram_tensor("CBT", [C, C], F32, kind="ExternalInput").ap()
    scb_h = nc.dram_tensor("SCB", [1, C], F32, kind="ExternalInput").ap()
    bias_h = nc.dram_tensor("BIAS", [1, C], F32, kind="ExternalInput").ap()
    scale_h = nc.dram_tensor("ROPE", [128, T], BF16, kind="ExternalInput").ap()
    masks_h = nc.dram_tensor("MASKS", [128, 2560], BF16, kind="ExternalInput").ap()
    c127_h = nc.dram_tensor("C127", [1, 128], F32, kind="ExternalInput").ap()
    ones_h = nc.dram_tensor("ONES", [1, 128], F32, kind="ExternalInput").ap()
    onesbf_h = nc.dram_tensor("ONESBF", [1, 128], BF16, kind="ExternalInput").ap()
    ident_h = nc.dram_tensor("IDENT", [128, 128], BF16, kind="ExternalInput").ap()
    y_hbm = nc.dram_tensor("y", [T, C], F32, kind="ExternalOutput").ap()
    qt_kind = "ExternalOutput" if debug else "Internal"
    qt_hbm = nc.dram_tensor("qt", [H, DQK, T], BF16, kind=qt_kind).ap()
    dbg = {}
    if debug:
        dbg["xnT"] = nc.dram_tensor("d_xnT", [128, 16384], BF16, kind="ExternalOutput").ap()
        dbg["ckvT"] = nc.dram_tensor("d_ckvT", [128, 4096], BF16, kind="ExternalOutput").ap()
        dbg["krT"] = nc.dram_tensor("d_krT", [32, 2048], BF16, kind="ExternalOutput").ap()
        dbg["yT"] = nc.dram_tensor("d_yT", [128, 16384], BF16, kind="ExternalOutput").ap()
        dbg["ktb0"] = nc.dram_tensor("d_ktb0", [96, 2048], BF16, kind="ExternalOutput").ap()
        dbg["wdqT"] = nc.dram_tensor("d_wdqT", [128, 32768], BF16, kind="ExternalOutput").ap()
        dbg["weff"] = nc.dram_tensor("d_weff", [128, 12288], BF16, kind="ExternalOutput").ap()
        dbg["ktb1"] = nc.dram_tensor("d_ktb1", [96, 2048], BF16, kind="ExternalOutput").ap()
        dbg["vb0"] = nc.dram_tensor("d_vb0", [128, 2048], BF16, kind="ExternalOutput").ap()
        dbg["vb1"] = nc.dram_tensor("d_vb1", [128, 2048], BF16, kind="ExternalOutput").ap()

    b = Builder()

    with ExitStack() as ctx:
        def chan(name):
            return Chan(ctx.enter_context(nc.semaphore(name)))
        wq = chan("wq")
        sq = chan("sq")
        xq = chan("xq")
        pq = chan("pq")
        vqA = chan("vqA")
        vqB = chan("vqB")
        vqC = chan("vqC")
        vqD = chan("vqD")
        aq = chan("aq")
        st = chan("st")
        pe = chan("pe")
        act = chan("act")
        dve = chan("dve")
        pool = chan("pool")
        ALLCH = [wq, sq, xq, pq, vqA, vqB, vqC, vqD, aq, st, pe, act, dve, pool]

        block = ctx.enter_context(nc.Block())

        def flush():
            for eng in ENGINES:
                fns = b.plan[eng]
                if not fns:
                    continue
                b.plan[eng] = []

                def section(e, fns=fns):
                    for f in fns:
                        f(e)
                getattr(block, eng)(section)

        xnT = ctx.enter_context(nc.sbuf_tensor("xnT", [128, 16384], BF16))
        ckvT = ctx.enter_context(nc.sbuf_tensor("ckvT", [128, 4096], BF16))
        krT = ctx.enter_context(nc.sbuf_tensor("krT", [32, 2048], BF16))
        rope = ctx.enter_context(nc.sbuf_tensor("rope", [128, 2048], BF16))
        c127 = ctx.enter_context(nc.sbuf_tensor("c127", [1, 128], F32))
        onesbf = ctx.enter_context(nc.sbuf_tensor("onesbf", [1, 128], BF16))
        ident = ctx.enter_context(nc.sbuf_tensor("ident", [128, 128], BF16))
        x_sb = ctx.enter_context(nc.sbuf_tensor("x_sb", [128, 4096], F32))
        ktb = [None, None, None]
        vb = [ctx.enter_context(nc.sbuf_tensor("vb0", [128, 2048], BF16)),
              ctx.enter_context(nc.sbuf_tensor("vb1", [128, 2048], BF16)),
              None]
        vstage = ctx.enter_context(nc.sbuf_tensor("vstage", [128, 2048], BF16))

        xld_at = {}
        for g0 in range(3):
            b.dma("sync", lambda e, g0=g0: e.dma_start(
                out=x_sb[:, g0 * 1024:(g0 + 1) * 1024],
                in_=x_hbm[g0 * 128:(g0 + 1) * 128, :]), xq)
            xld_at[g0] = xq.n

        b.dma("sync", lambda e: e.dma_start(out=c127[:], in_=c127_h), sq)
        b.dma("sync", lambda e: e.dma_start(out=onesbf[:], in_=onesbf_h), sq)
        b.dma("sync", lambda e: e.dma_start(out=ident[:], in_=ident_h), sq)
        ident_sq = sq.n
        b.dma("sync", lambda e: e.dma_start(out=rope[:], in_=scale_h), sq)
        const_sq = sq.n

        kt_rd = [[0] * 4 for _ in range(3)]
        vb_rd = [[[] for _ in range(16)] for _ in range(3)]
        vstage_free = [[], [], [], []]
        uk_ld = [0]
        ckv_rdy = [0]

        def emit_build_tile(p, k, prev_pe, psy, psy_free, wuk, wuv, wready):
            bank = k % 2
            bufA, bufB = (2 * p) % 3, (2 * p + 1) % 3
            b.wait("tensor", wready[0], wready[1])
            b.wait("tensor", dve, ckv_rdy[0])
            b.wait("tensor", pe, prev_pe)
            b.wait("tensor", dve, psy_free[bank])
            if k >= 4:
                t4 = k - 4
                for c in range(2):
                    b.op("tensor", lambda e, bank=bank, c=c, p=p, t4=t4, psy=psy, wuk=wuk: e.matmul(
                        psy(bank),
                        wuk[:, c * 1024 + 128 * p:c * 1024 + 128 * p + 128],
                        ckvT[:, c * 2048 + t4 * 512:c * 2048 + t4 * 512 + 512],
                        start=(c == 0), stop=(c == 1)),
                        inc=(pe if c == 1 else None))
                mmn = pe.n
                b.wait("vector", pe, mmn)
                b.op("vector", lambda e, bank=bank, bufA=bufA, t4=t4, psy=psy: e.tensor_copy(
                    out=ktb[bufA][0:64, t4 * 512:(t4 + 1) * 512],
                    in_=psy(bank)[0:64, :]), inc=dve)
                b.op("vector", lambda e, bank=bank, bufB=bufB, t4=t4, psy=psy: e.tensor_copy(
                    out=ktb[bufB][0:64, t4 * 512:(t4 + 1) * 512],
                    in_=psy(bank)[64:128, :]), inc=dve)
                psy_free[bank] = dve.n
                kt_rd[bufA][t4] = kt_rd[bufB][t4] = dve.n
            else:
                t4 = k
                for c in range(2):
                    b.op("tensor", lambda e, bank=bank, c=c, p=p, t4=t4, psy=psy, wuv=wuv: e.matmul(
                        psy(bank),
                        wuv[:, c * 1024 + 128 * p:c * 1024 + 128 * p + 128],
                        ckvT[:, c * 2048 + t4 * 512:c * 2048 + t4 * 512 + 512],
                        start=(c == 0), stop=(c == 1)),
                        inc=(pe if c == 1 else None))
                mmn = pe.n
                b.wait("vector", pe, mmn)
                for chn, v in vstage_free[t4]:
                    b.wait("vector", chn, v)
                b.op("vector", lambda e, bank=bank, t4=t4, psy=psy: e.tensor_copy(
                    out=vstage[:, t4 * 512:(t4 + 1) * 512],
                    in_=psy(bank)), inc=dve)
                psy_free[bank] = dve.n
                evn = dve.n
                b.wait("sync", dve, evn)
                b.wait("sync", pe, prev_pe)
                vf = []
                for tt in range(4 * t4, 4 * t4 + 4):
                    lo = (tt - 4 * t4) * 128 + t4 * 512
                    b.dma("sync", lambda e, bufA=bufA, tt=tt, lo=lo: e.dma_start(
                        out=vb[bufA][:, tt * 128:tt * 128 + 64],
                        in_=vstage[0:64, lo:lo + 128], transpose=True), vqA)
                    b.dma("sync", lambda e, bufB=bufB, tt=tt, lo=lo: e.dma_start(
                        out=vb[bufB][:, tt * 128:tt * 128 + 64],
                        in_=vstage[64:128, lo:lo + 128], transpose=True), vqB)
                    vb_rd[bufA][tt] = [(vqA, vqA.n)]
                    vb_rd[bufB][tt] = [(vqB, vqB.n)]
                    vf.extend([(vqA, vqA.n), (vqB, vqB.n)])
                vstage_free[t4] = vf

        build_sched = []
        init_settle = []

        with ExitStack() as pctx:
            xn_tok = pctx.enter_context(nc.sbuf_tensor("xn_tok", [128, 2048], BF16))
            sq_scr = pctx.enter_context(nc.sbuf_tensor("sq_scr", [128, 1024], F32))
            stats = pctx.enter_context(nc.sbuf_tensor("stats", [128, 32], F32))
            ptr = pctx.enter_context(nc.psum_tensor("ptr", [128, 2048], BF16))
            wdq_b = pctx.enter_context(nc.sbuf_tensor("wdq_b", [128, 2048], BF16))
            wdqT = pctx.enter_context(nc.sbuf_tensor("wdqT", [128, 32768], BF16))
            wu_mt = pctx.enter_context(nc.sbuf_tensor("wu_mt", [128, 12288], BF16))
            weff_sb = pctx.enter_context(nc.sbuf_tensor("weff_sb", [128, 12288], BF16))
            wdkv_b = pctx.enter_context(nc.sbuf_tensor("wdkv_b", [128, 2048], BF16))
            wkr_b = pctx.enter_context(nc.sbuf_tensor("wkr_b", [128, 256], BF16))
            stage = pctx.enter_context(nc.sbuf_tensor("stage", [128, 1024], BF16))
            ps_main = pctx.enter_context(nc.psum_tensor("ps_main", [128, 2048], F32))
            ps_init = pctx.enter_context(nc.psum_tensor("ps_init", [128, 1024], F32))

            def psy_p(bank):
                return ps_init[:, bank * 512:(bank + 1) * 512]
            psy_free_p = [0, 0]
            cast_slack = [0, 0]
            wdkv_reuse = [0]
            wkr_reuse = [0]

            pp_free = [(dve, 0)] * 4
            x_free = [0, 0, 0, 0]
            x_free_act = [0, 0, 0, 0]
            xn_free_pe = [0, 0]
            stats_free = [0, 0, 0, 0]
            ptr_free = [0, 0]
            sqscr_free = [0, 0]
            stage_free = [0, 0]
            cq_slot = [0]
            stg = [0]
            wdq_use = [0, 0]
            wuq_use = [0, 0]

            def pslot():
                s = cq_slot[0] % 4
                cq_slot[0] += 1
                return s

            def stage_slot():
                return 0

            def issue_xload_p(gt):
                par = gt % 4
                b.wait("sync", dve, x_free[par])
                b.wait("sync", act, x_free_act[par])
                b.dma("sync", lambda e, gt=gt, par=par: e.dma_start(
                    out=x_sb[:, par * 1024:(par + 1) * 1024],
                    in_=x_hbm[gt * 128:(gt + 1) * 128, :]), xq)
                xld_at[gt] = xq.n

            def issue_xdummy_p():
                b.wait("sync", dve, x_free[0])
                b.wait("sync", act, x_free_act[0])
                b.dma("sync", lambda e: e.dma_start(
                    out=x_sb[:, 0:1024], in_=x_hbm[0:128, :]), xq)
                xld_at[16] = xq.n

            xnT_rth = {0: [None, None], 1: [None, None]}
            xnT_rdy = {}
            st1 = {}

            if True:
                def ln_stage1(tt, tc):
                    gt = tc * 8 + tt
                    par = tt % 4
                    so = par * 8
                    nxt = gt + 3
                    if nxt <= 15:
                        issue_xload_p(nxt)
                    elif nxt == 16:
                        issue_xdummy_p()
                    ld = xld_at.get(gt + 2)
                    if ld is None:
                        ld = xld_at[gt + 1]
                    b.wait("vector", sq, const_sq)
                    b.wait("scalar", sq, const_sq)
                    b.wait("vector", xq, ld)
                    b.wait("vector", dve, stats_free[par])
                    b.op("vector", lambda e, par=par, so=so: e.reduce_sum(
                        out=stats[:, so:so + 1],
                        in_=x_sb[:, par * 1024:(par + 1) * 1024], axis=AX.X), inc=dve)
                    s1_n = dve.n
                    b.wait("scalar", xq, ld)
                    b.wait("scalar", dve, sqscr_free[0])
                    b.op("scalar", lambda e, par=par: e.activation(
                        out=sq_scr[:],
                        in_=x_sb[:, par * 1024:(par + 1) * 1024],
                        func=AF.Square), inc=act)
                    x_free_act[par] = act.n
                    sq_n = act.n
                    b.wait("vector", act, sq_n)
                    b.op("vector", lambda e, so=so: e.reduce_sum(
                        out=stats[:, so + 1:so + 2],
                        in_=sq_scr[:],
                        axis=AX.X), inc=dve)
                    sqscr_free[0] = dve.n
                    ss_n = dve.n
                    b.wait("scalar", dve, s1_n)
                    b.op("scalar", lambda e, so=so: e.activation(
                        out=stats[:, so + 2:so + 3], in_=stats[:, so:so + 1],
                        func=AF.Copy, scale=-1.0 / 1024.0), inc=act)
                    st1[tt] = (par, so, ss_n, act.n)

                def ln_stage2(tt, tc):
                    par, so, ss_n, nm_n = st1.pop(tt)
                    xpar = tt % 2
                    b.wait("vector", act, nm_n)
                    b.op("vector", lambda e, so=so: e.tensor_scalar(
                        out=stats[:, so + 4:so + 5], in0=stats[:, so + 2:so + 3],
                        scalar1=stats[:, so + 2:so + 3], scalar2=1e-5,
                        op0=ALU.mult, op1=ALU.subtract), inc=dve)
                    m2_n = dve.n
                    b.wait("vector", dve, max(ss_n, m2_n))
                    b.op("vector", lambda e, so=so: e.tensor_scalar(
                        out=stats[:, so + 5:so + 6], in0=stats[:, so + 1:so + 2],
                        scalar1=1.0 / 1024.0, scalar2=stats[:, so + 4:so + 5],
                        op0=ALU.mult, op1=ALU.subtract), inc=dve)
                    var_n = dve.n
                    b.wait("scalar", dve, var_n)
                    b.op("scalar", lambda e, so=so: e.activation(
                        out=stats[:, so + 6:so + 7], in_=stats[:, so + 5:so + 6],
                        func=AF.Ln), inc=act)
                    b.wait("scalar", act, act.n)
                    b.op("scalar", lambda e, so=so: e.activation(
                        out=stats[:, so + 3:so + 4], in_=stats[:, so + 6:so + 7],
                        func=AF.Exp, scale=-0.5), inc=act)
                    rstd_n = act.n
                    b.wait("vector", act, rstd_n)
                    b.wait("vector", pe, xn_free_pe[xpar])
                    b.op("vector", lambda e, par=par, so=so, xpar=xpar: e.tensor_scalar(
                        out=xn_tok[:, xpar * 1024:(xpar + 1) * 1024],
                        in0=x_sb[:, par * 1024:(par + 1) * 1024],
                        scalar1=stats[:, so + 2:so + 3], scalar2=stats[:, so + 3:so + 4],
                        op0=ALU.add, op1=ALU.mult), inc=dve)
                    x_free[par] = dve.n
                    xn_n = dve.n
                    stats_free[par] = dve.n
                    b.wait("tensor", sq, const_sq)
                    b.wait("tensor", dve, xn_n)
                    for c in range(8):
                        ps = c % 2
                        b.wait("tensor", dve, ptr_free[ps])
                        b.op("tensor", lambda e, c=c, ps=ps, xpar=xpar: e.transpose(
                            ptr[:, ps * 1024:ps * 1024 + 128],
                            xn_tok[:, xpar * 1024 + c * 128:xpar * 1024 + c * 128 + 128],
                            ident[:]), inc=pe)
                        trn = pe.n
                        b.wait("vector", pe, trn)
                        b.op("vector", lambda e, c=c, ps=ps, tc=tc, tt=tt: e.tensor_copy(
                            out=xnT[:, c * 2048 + tc * 1024 + tt * 128:
                                    c * 2048 + tc * 1024 + tt * 128 + 128],
                            in_=ptr[:, ps * 1024:ps * 1024 + 128]), inc=dve)
                        ptr_free[ps] = dve.n
                    xn_free_pe[xpar] = pe.n
                    if tt == 3:
                        xnT_rth[tc][0] = dve.n
                    if tt == 7:
                        xnT_rth[tc][1] = dve.n
                        xnT_rdy[tc] = dve.n

                def ln_piece(k, tc):
                    if k >= 1:
                        ln_stage2(k - 1, tc)
                    if k < 8:
                        ln_stage1(k, tc)

            def emit_ckv_kr(tc):
                if tc == 1:
                    b.wait("gpsimd", pe, wdkv_reuse[0])
                b.dma("gpsimd", lambda e: e.dma_start(
                    out=wdkv_b.rearrange("p (c m) -> p c m", m=256),
                    in_=w_dkv.rearrange("(c p) m -> p c m", p=128)), wq)
                ld = wq.n
                for m in range(2):
                    for th in range(2):
                        s = pslot()
                        b.wait("tensor", wq, ld)
                        b.wait("tensor", dve, xnT_rdy[tc])
                        b.wait("tensor", *pp_free[s])
                        for c in range(8):
                            b.op("tensor", lambda e, s=s, c=c, m=m, tc=tc, th=th: e.matmul(
                                ps_main[:, s * 512:(s + 1) * 512],
                                wdkv_b[:, c * 256 + m * 128:c * 256 + m * 128 + 128],
                                xnT[:, c * 2048 + tc * 1024 + th * 512:
                                    c * 2048 + tc * 1024 + th * 512 + 512],
                                start=(c == 0), stop=(c == 7)),
                                inc=(pe if c == 7 else None))
                        mmn = pe.n
                        b.wait("vector", pe, mmn)
                        b.op("vector", lambda e, s=s, m=m, tc=tc, th=th: e.tensor_copy(
                            out=ckvT[:, m * 2048 + tc * 1024 + th * 512:
                                     m * 2048 + tc * 1024 + th * 512 + 512],
                            in_=ps_main[:, s * 512:(s + 1) * 512]), inc=dve)
                        pp_free[s] = (dve, dve.n)
                wdkv_reuse[0] = pe.n
                ckv_rdy[0] = dve.n

                if tc == 1:
                    b.wait("gpsimd", pe, wkr_reuse[0])
                b.dma("gpsimd", lambda e: e.dma_start(
                    out=wkr_b.rearrange("p (c m) -> p c m", m=32),
                    in_=w_kr.rearrange("(c p) m -> p c m", p=128)), wq)
                ld = wq.n
                for th in range(2):
                    s = pslot()
                    b.wait("tensor", wq, ld)
                    b.wait("tensor", *pp_free[s])
                    for c in range(8):
                        b.op("tensor", lambda e, s=s, c=c, tc=tc, th=th: e.matmul(
                            ps_main[0:32, s * 512:(s + 1) * 512],
                            wkr_b[:, c * 32:c * 32 + 32],
                            xnT[:, c * 2048 + tc * 1024 + th * 512:
                                c * 2048 + tc * 1024 + th * 512 + 512],
                            start=(c == 0), stop=(c == 7)),
                            inc=(pe if c == 7 else None))
                    mmn = pe.n
                    off = tc * 1024 + th * 512
                    b.wait("vector", pe, mmn)
                    b.wait("vector", sq, const_sq)
                    b.op("vector", lambda e, s=s, off=off: e.tensor_tensor(
                        out=krT[0:32, off:off + 512],
                        in0=ps_main[0:32, s * 512:(s + 1) * 512],
                        in1=rope[0:32, off:off + 512], op=ALU.mult), inc=dve)
                    pp_free[s] = (dve, dve.n)
                wkr_reuse[0] = pe.n

            for i3 in range(2):
                b.op("gpsimd", lambda e, i3=i3: e.memset(vb[i3][:], 1.0),
                     inc=pool)
            vb_init_n = pool.n

            wdq_ld = [0] * 32
            wu_ld = {}
            wu_war = [0] * 32
            wdq_war = [0, 0]
            wdqT_rd = [0] * 32
            weff_mm = [[0] * 4 for _ in range(2)]

            def emit_wu_load(mt, d):
                b.wait("gpsimd", pe, wu_war[d])
                lo, hi = mt * 384, mt * 384 + 384
                spans = []
                if lo < 1024:
                    spans.append((w_uq, lo, min(hi, 1024), 0))
                if hi > 1024:
                    spans.append((w_qr, max(lo, 1024) - 1024, hi - 1024,
                                  max(lo, 1024) - lo))
                for srcw, c0s, c1s, off in spans:
                    b.dma("gpsimd", lambda e, srcw=srcw, c0s=c0s, c1s=c1s, off=off, d=d: e.dma_start(
                        out=wu_mt[:, d * 384 + off:d * 384 + off + (c1s - c0s)],
                        in_=srcw[d * 128:(d + 1) * 128, c0s:c1s]), wq)
                wu_ld[(mt, d)] = wq.n
                if d == 31:
                    for _ in range(2):
                        b.dma("gpsimd", lambda e: e.dma_start(
                            out=onesbf[:], in_=onesbf_h), wq)

            def emit_weff_mms(mt, cbg, d):
                b.wait("tensor", wq, wu_ld[(mt, d)] + 32)
                b.wait("tensor", dve, wdqT_rd[d])
                for cb4 in range(4):
                    cb = cbg * 4 + cb4
                    if d == 0:
                        b.wait("tensor", *pp_free[cb4])
                    b.op("tensor", lambda e, cb=cb, cb4=cb4, d=d: e.matmul(
                        ps_main[:, cb4 * 512:cb4 * 512 + 384],
                        wdqT[:, d * 1024 + cb * 128:d * 1024 + cb * 128 + 128],
                        wu_mt[:, d * 384:(d + 1) * 384],
                        start=(d == 0), stop=(d == 31)),
                        inc=(pe if (d == 31 or (cbg == 1 and cb4 == 3)) else None))
                    if d == 31:
                        weff_mm[cbg][cb4] = pe.n

            def emit_weff_evac(mt, cbg):
                for cb4 in range(4):
                    cb = cbg * 4 + cb4
                    b.wait("vector", pe, weff_mm[cbg][cb4])
                    b.op("vector", lambda e, mt=mt, cb=cb, cb4=cb4: e.tensor_copy(
                        out=weff_sb[:, cb * 1536 + mt * 384:cb * 1536 + (mt + 1) * 384],
                        in_=ps_main[:, cb4 * 512:cb4 * 512 + 384]), inc=dve)
                    pp_free[cb4] = (dve, dve.n)

            for tc9 in range(2):
                for k9 in range(9):
                    build_sched.append(lambda k9=k9, tc9=tc9: ln_piece(k9, tc9))

            for i in range(32):
                par = i % 2
                b.wait("gpsimd", pe, wdq_war[par])
                b.dma("gpsimd", lambda e, i=i, par=par: e.dma_start(
                    out=wdq_b[:, par * 1024:(par + 1) * 1024]
                        .rearrange("p (c d) -> p c d", d=128),
                    in_=w_dq[:, i * 128:(i + 1) * 128]
                        .rearrange("(c p) d -> p c d", p=128)), wq)
                wdq_ld[i] = wq.n
                emit_wu_load(0, i)
                ps = i % 2
                b.wait("tensor", wq, wu_ld[(0, i)] + (16 if i == 31 else 32))
                b.wait("tensor", dve, ptr_free[ps])
                b.wait("tensor", sq, ident_sq)
                for cb in range(8):
                    b.op("tensor", lambda e, cb=cb, ps=ps, par=par: e.transpose(
                        ptr[:, ps * 1024 + cb * 128:ps * 1024 + cb * 128 + 128],
                        wdq_b[:, par * 1024 + cb * 128:par * 1024 + cb * 128 + 128],
                        ident[:]), inc=(pe if cb == 7 else None))
                wdq_war[par] = pe.n
                b.wait("vector", pe, pe.n)
                b.op("vector", lambda e, i=i, ps=ps: e.tensor_copy(
                    out=wdqT[:, i * 1024:(i + 1) * 1024],
                    in_=ptr[:, ps * 1024:(ps + 1) * 1024]), inc=dve)
                wdqT_rd[i] = dve.n
                ptr_free[ps] = dve.n
                emit_weff_mms(0, 0, i)
                if build_sched:
                    build_sched.pop(0)()
            emit_weff_evac(0, 0)
            emit_ckv_kr(0)

            for mt in range(4):
                for cbg in range(2):
                    if mt == 0 and cbg == 0:
                        continue
                    for d in range(32):
                        emit_weff_mms(mt, cbg, d)
                        if cbg == 1:
                            wu_war[d] = pe.n
                            if mt < 3:
                                emit_wu_load(mt + 1, d)
                    emit_weff_evac(mt, cbg)
            weff_rdy = dve.n

            emit_ckv_kr(1)

            b.wait("scalar", dve, max(x_free))
            b.wait("scalar", xq, xld_at[16])
            for src, lo in ((w_uk, 0), (w_uk, 1), (w_uv, 2), (w_uv, 3)):
                b.dma("scalar", lambda e, src=src, lo=lo: e.dma_start(
                    out=x_sb[:, lo * 1024:(lo + 1) * 1024],
                    in_=src[(lo % 2) * 128:(lo % 2) * 128 + 128, :]), aq)
                if lo == 2:
                    cast_slack[0] = aq.n
            b.dma("scalar", lambda e: e.dma_start(out=onesbf[:], in_=onesbf_h), aq)
            cast_slack[1] = aq.n
            b.wait("scalar", pe, pe.n)
            b.wait("scalar", aq, cast_slack[1])
            for hh in range(2):
                b.op("scalar", lambda e, hh=hh: e.activation(
                    out=wdqT[:, hh * 1024:(hh + 1) * 1024],
                    in_=x_sb[:, 2048 + hh * 1024:2048 + (hh + 1) * 1024],
                    func=AF.Copy), inc=act)
            uk_ld[0] = act.n
            build_sched.append(lambda: b.wait("sync", pool, vb_init_n))
            for kk in range(4):
                build_sched.append(lambda kk=kk: emit_build_tile(
                    0, kk, 0, psy_p, psy_free_p,
                    wdqT[:, 0:2048], wdqT[:, 0:2048],
                    (act, uk_ld[0])))

            qstg = [(stage, 0), (xn_tok, 0)]
            qstg_free = [0, 0]
            for tc in range(2):
                boff = tc * 1024
                for j in range(12):
                    g = tc * 12 + j
                    sl = g % 2
                    qt_, qo_ = qstg[sl]
                    gslots = []
                    for th in range(2):
                        s = pslot()
                        gslots.append(s)
                        b.wait("tensor", dve, weff_rdy)
                        b.wait("tensor", *pp_free[s])
                        b.wait("tensor", dve, xnT_rdy[tc])
                        for c in range(8):
                            b.op("tensor", lambda e, s=s, c=c, j=j, tc=tc, th=th: e.matmul(
                                ps_main[:, s * 512:(s + 1) * 512],
                                weff_sb[:, c * 1536 + j * 128:c * 1536 + j * 128 + 128],
                                xnT[:, c * 2048 + tc * 1024 + th * 512:
                                    c * 2048 + tc * 1024 + th * 512 + 512],
                                start=(c == 0), stop=(c == 7)),
                                inc=(pe if c == 7 else None))
                        mmn = pe.n
                        off = tc * 1024 + th * 512
                        if j < 8:
                            b.wait("scalar", pe, mmn)
                            b.wait("scalar", aq, qstg_free[sl])
                            b.op("scalar", lambda e, s=s, th=th, qt_=qt_, qo_=qo_: e.activation(
                                out=qt_[:, qo_ + th * 512:qo_ + th * 512 + 512],
                                in_=ps_main[:, s * 512:(s + 1) * 512],
                                func=AF.Copy), inc=act)
                        else:
                            b.wait("vector", pe, mmn)
                            b.wait("vector", aq, qstg_free[sl])
                            b.wait("vector", sq, const_sq)
                            b.op("vector", lambda e, s=s, th=th, off=off, qt_=qt_, qo_=qo_: e.tensor_tensor(
                                out=qt_[:, qo_ + th * 512:qo_ + th * 512 + 512],
                                in0=ps_main[:, s * 512:(s + 1) * 512],
                                in1=rope[:, off:off + 512],
                                op=ALU.mult), inc=dve)
                            pp_free[s] = (dve, dve.n)
                    if j < 8:
                        b.wait("scalar", act, act.n)
                        b.dma("scalar", lambda e, j=j, boff=boff, qt_=qt_, qo_=qo_: e.dma_start(
                            out=qt_hbm[2 * j:2 * j + 2, 0:64, boff:boff + 1024],
                            in_=qt_[:, qo_:qo_ + 1024]), aq)
                        for s2 in gslots:
                            pp_free[s2] = (aq, aq.n)
                    else:
                        cpn = dve.n
                        jj = j - 8
                        b.wait("scalar", dve, cpn)
                        b.dma("scalar", lambda e, jj=jj, boff=boff, qt_=qt_, qo_=qo_: e.dma_start(
                            out=qt_hbm[4 * jj:4 * jj + 4, 64:96, boff:boff + 1024],
                            in_=qt_[:, qo_:qo_ + 1024]), aq)
                    qstg_free[sl] = aq.n
                    if build_sched:
                        build_sched.pop(0)()

            assert not build_sched
            init_settle.extend([(vqA, vqA.n), (vqB, vqB.n), (dve, dve.n)])

            if debug:
                b.wait("sync", dve, dve.n)
                b.wait("sync", act, act.n)
                b.dma("sync", lambda e: e.dma_start(out=dbg["weff"], in_=weff_sb[:]), sq)
                b.dma("sync", lambda e: e.dma_start(out=dbg["xnT"], in_=xnT[:]), sq)
                b.dma("sync", lambda e: e.dma_start(out=dbg["ckvT"], in_=ckvT[:]), sq)
                b.dma("sync", lambda e: e.dma_start(out=dbg["krT"], in_=krT[:]), sq)
            b.barrier(ALLCH)
            flush()

        if phases < 2:
            return nc

        yT = ctx.enter_context(nc.sbuf_tensor("yT", [128, 16384], BF16))
        wo_sb = ctx.enter_context(nc.sbuf_tensor("wo_sb", [128, 8192], BF16))
        cbt_sb = ctx.enter_context(nc.sbuf_tensor("cbt_sb", [128, 8192], BF16))
        with ExitStack() as actx:
            wuk_sb = actx.enter_context(nc.sbuf_tensor("wuk_sb", [128, 2048], BF16))
            wuv_sb = actx.enter_context(nc.sbuf_tensor("wuv_sb", [128, 2048], BF16))
            ktb[0] = actx.enter_context(nc.sbuf_tensor("ktb0", [96, 2048], BF16))
            ktb[1] = actx.enter_context(nc.sbuf_tensor("ktb1", [96, 2048], BF16))
            ktb[2] = actx.enter_context(nc.sbuf_tensor("ktb2", [96, 2048], BF16))
            vb[2] = actx.enter_context(nc.sbuf_tensor("vb2", [128, 2048], BF16))
            qtb = [actx.enter_context(nc.sbuf_tensor(f"qtb{i}", [96, 2048], BF16))
                   for i in range(2)]
            masks = actx.enter_context(nc.sbuf_tensor("masks", [128, 2560], BF16))
            pt = actx.enter_context(nc.sbuf_tensor("pt", [128, 2048], BF16))
            ln_tmp = actx.enter_context(nc.sbuf_tensor("ln_tmp", [64, 1024], F32))
            rb = actx.enter_context(nc.sbuf_tensor("rb", [64, 1024], F32))
            pss_all = actx.enter_context(nc.psum_tensor("pss_all", [128, 2048], F32))

            def pss_v(s, bank):
                return pss_all[:, (2 * bank + s) * 512:(2 * bank + s + 1) * 512]
            ps_yp = [actx.enter_context(nc.psum_tensor(f"ps_yp{q}", [128, 1024], F32))
                     for q in range(2)]

            def psy(g):
                return ps_yp[g // 2][:, (g % 2) * 512:(g % 2 + 1) * 512]

            b.dma("gpsimd", lambda e: e.dma_start(
                out=wuv_sb.rearrange("p (c m) -> p c m", m=1024),
                in_=w_uv.rearrange("(c p) m -> p c m", p=128)), wq)
            uv_ld_a = wq.n
            b.dma("gpsimd", lambda e: e.dma_start(
                out=wo_sb.rearrange("p (f m) -> p f m", m=1024),
                in_=w_o.rearrange("(f p) m -> p f m", p=128)), wq)
            b.dma("gpsimd", lambda e: e.dma_start(
                out=cbt_sb.rearrange("p (c m) -> p c m", m=1024),
                in_=cbt.rearrange("(c p) m -> p c m", p=128)), wq)
            fw_ld = wq.n

            for hh in range(2):
                b.op("scalar", lambda e, hh=hh: e.activation(
                    out=wuk_sb[:, hh * 1024:(hh + 1) * 1024],
                    in_=x_sb[:, hh * 1024:(hh + 1) * 1024],
                    func=AF.Copy), inc=act)
            uk_cast = act.n

            b.op("gpsimd", lambda e: e.memset(vb[2][:], 1.0), inc=pool)
            amem = pool.n
            for i3 in range(3):
                b.op("vector", lambda e, i3=i3: e.tensor_copy(
                    out=ktb[i3][64:96, :], in_=krT[:]), inc=dve)
            ainit = dve.n

            b.dma("sync", lambda e: e.dma_start(out=masks[:], in_=masks_h), sq)
            masks_ld = sq.n
            b.wait("tensor", sq, masks_ld)
            b.dma("scalar", lambda e: e.dma_start(out=qtb[0][:], in_=qt_hbm[0]), pq)
            qt_ld = [pq.n, 0]
            b.dma("scalar", lambda e: e.dma_start(out=qtb[1][:], in_=qt_hbm[1]), pq)
            qt_ld[1] = pq.n

            war_head = [0] * 17
            psy_free = [0, 0, 0, 0]
            exp_bank = [[0, 0], [0, 0]]
            rb_free = [0]

            STREAM = [[(0, j) for j in range(4)] + [(3, j) for j in range(16)],
                      [(1, j) for j in range(8)] + [(2, j) for j in range(12)]]

            pt_pv = [[0, 0], [0, 0]]
            fin_ready = [0, 0]

            def emit_scores(h, s, g, j, par, bank):
                boundary = (g == j // 4)
                c0 = max(0, j * 128 - 512 * g)
                b.wait("tensor", dve, kt_rd[par][j // 4])
                b.op("tensor", lambda e, s=s, g=g, j=j, par=par, bank=bank, c0=c0: e.matmul(
                    pss_v(s, bank)[:, c0:512],
                    ktb[par][0:96, j * 128:j * 128 + 128],
                    qtb[h % 2][0:96, g * 512 + c0:(g + 1) * 512],
                    start=True, stop=not boundary),
                    inc=(None if boundary else pe))
                if boundary:
                    r = j % 4
                    b.op("tensor", lambda e, s=s, bank=bank, r=r, c0=c0: e.matmul(
                        pss_v(s, bank)[:, c0:c0 + 128],
                        ident[:], masks[:, r * 512 + c0:r * 512 + c0 + 128],
                        start=False, stop=True), inc=pe)
                return c0

            def emit_exp2(bank, c0A):
                smm = pe.n
                b.wait("scalar", pe, smm)
                b.wait("scalar", pe, max(pt_pv[0][bank], pt_pv[1][bank]))
                lo = 2 * bank * 512 + c0A
                hi = (2 * bank + 2) * 512
                b.op("scalar", lambda e, lo=lo, hi=hi: e.activation(
                    out=pt[:, lo:hi], in_=pss_all[:, lo:hi],
                    func=AF.Exp, scale=INV_SQRT), inc=act)
                exp_bank[0][bank] = exp_bank[1][bank] = act.n
                return act.n

            def emit_pv(h, s, g, j, par, hq, bank, expn):
                reg = (2 * bank + s) * 512
                c0 = max(0, j * 128 - 512 * g)
                for chn, v in vb_rd[par][j]:
                    b.wait("tensor", chn, v)
                if j == 0:
                    b.wait("tensor", dve, psy_free[g])
                b.op("tensor", lambda e, g=g, j=j, par=par, reg=reg, c0=c0: e.matmul(
                    psy(g)[:, c0:512],
                    vb[par][:, j * 128:(j + 1) * 128],
                    pt[:, reg + c0:reg + 512],
                    start=(j == 0), stop=(j == 4 * g + 3)), inc=pe)
                pt_pv[s][bank] = pe.n
                if j == 4 * g + 3 and (g % 2 == 1 or g == 2):
                    fin_ready[g // 2] += 1
                    if g // 2 == 0 or fin_ready[1] == 2:
                        emit_finish(h, g // 2, hq)

            finish_q = []

            def emit_finish(h, q, hq):
                psn = pe.n

                def p_ln(q=q, psn=psn):
                    b.wait("scalar", pe, psn)
                    b.wait("scalar", dve, rb_free[0])
                    b.op("scalar", lambda e, q=q: e.activation(
                        out=ln_tmp[:],
                        in_=ps_yp[q][64:128, :],
                        func=AF.Ln), inc=act)

                def p_exp():
                    b.wait("scalar", act, act.n)
                    b.op("scalar", lambda e: e.activation(
                        out=rb[:], in_=ln_tmp[:],
                        func=AF.Exp, scale=-1.0), inc=act)

                def p_mult(q=q, h=h, hq=hq):
                    b.wait("vector", act, act.n)
                    b.op("vector", lambda e, q=q, h=h, hq=hq: e.tensor_tensor(
                        out=yT[64 * (h % 2):64 * (h % 2) + 64,
                               hq + q * 1024:hq + (q + 1) * 1024],
                        in0=ps_yp[q][0:64, :], in1=rb[:], op=ALU.mult), inc=dve)
                    psy_free[2 * q] = psy_free[2 * q + 1] = dve.n
                    rb_free[0] = dve.n

                finish_q.extend([p_ln, p_exp, p_mult])

            for kk in range(4, 8):
                emit_build_tile(0, kk, 0, psy, psy_free,
                                wuk_sb[:], wuk_sb[:], (act, uk_cast))
            if debug:
                for chn, v in init_settle:
                    b.wait("sync", chn, v)
                b.dma("sync", lambda e: e.dma_start(out=dbg["ktb0"], in_=ktb[0][:]), sq)
                b.dma("sync", lambda e: e.dma_start(out=dbg["ktb1"], in_=ktb[1][:]), sq)
                b.dma("sync", lambda e: e.dma_start(out=dbg["vb0"], in_=vb[0][:]), sq)
                b.dma("sync", lambda e: e.dma_start(out=dbg["vb1"], in_=vb[1][:]), sq)
                for eng2 in ("tensor", "vector", "scalar"):
                    b.wait(eng2, sq, sq.n)

            for h in range(16):
                par = h % 3
                hq = (h // 2) * 2048
                b.wait("tensor", pq, qt_ld[h % 2])
                b.wait("tensor", dve, ainit)
                b.wait("tensor", pool, amem)
                if h < 2:
                    for chn, v in init_settle:
                        b.wait("tensor", chn, v)
                b.wait("vector", sq, const_sq)
                build_k = [0]
                fin_ready[0] = fin_ready[1] = 0
                pend = []
                for i in range(22):
                    for _ in range(2):
                        if finish_q:
                            finish_q.pop(0)()
                    if len(pend) == 2 or (i >= 20 and pend):
                        (pgA, pjA, pgB, pjB, pexp, pbank) = pend.pop(0)
                        b.wait("tensor", act, pexp)
                        emit_pv(h, 0, pgA, pjA, par, hq, pbank, pexp)
                        emit_pv(h, 1, pgB, pjB, par, hq, pbank, pexp)
                    if i < 20:
                        if i < 2:
                            b.wait("tensor", act, exp_bank[0][i % 2])
                        gA, jA = STREAM[0][i]
                        c0A = emit_scores(h, 0, gA, jA, par, i % 2)
                        gB, jB = STREAM[1][i]
                        emit_scores(h, 1, gB, jB, par, i % 2)
                        en = emit_exp2(i % 2, c0A)
                        pend.append((gA, jA, gB, jB, en, i % 2))
                    if i == 8 and h < 15:
                        npar2 = (h + 1) % 2
                        b.wait("scalar", pe, war_head[h - 1] if h >= 1 else 0)
                        b.dma("scalar", lambda e, h=h, npar2=npar2: e.dma_start(
                            out=qtb[npar2][:], in_=qt_hbm[h + 1]), pq)
                        qt_ld[npar2] = pq.n
                    if i >= 13 and (h % 2 == 1) and h < 15 and build_k[0] < 8:
                        kk2 = build_k[0]
                        if kk2 == 0:
                            b.wait("sync", pool, amem)
                        emit_build_tile((h + 1) // 2, kk2,
                                        war_head[h - 1] if h >= 1 else 0,
                                        psy, psy_free, wuk_sb[:], wuv_sb[:],
                                        (wq, uv_ld_a) if kk2 < 4 else
                                        (act, uk_cast))
                        build_k[0] += 1
                war_head[h] = pe.n
            while finish_q:
                finish_q.pop(0)()

            if debug:
                b.wait("sync", dve, dve.n)
                b.dma("sync", lambda e: e.dma_start(out=dbg["yT"], in_=yT[:]), sq)
            b.barrier(ALLCH)
            flush()

        if phases < 3:
            return nc

        with ExitStack() as fctx:
            scb_bc = fctx.enter_context(nc.sbuf_tensor("scb_bc", [128, 1024], F32))
            scb_row = fctx.enter_context(nc.sbuf_tensor("scb_row", [1, 1024], F32))
            bias_row = fctx.enter_context(nc.sbuf_tensor("bias_row", [1, 1024], F32))
            rr1 = fctx.enter_context(nc.sbuf_tensor("rr1", [1, 1024], F32))
            rr2 = fctx.enter_context(nc.sbuf_tensor("rr2", [1, 1024], F32))
            browbf = fctx.enter_context(nc.sbuf_tensor("browbf", [1, 1024], BF16))
            t1 = fctx.enter_context(nc.sbuf_tensor("t1", [128, 2048], F32))
            g_sb = fctx.enter_context(nc.sbuf_tensor("g_sb", [128, 2048], F32))
            t3 = fctx.enter_context(nc.sbuf_tensor("t3", [128, 2048], F32))
            out_sb = fctx.enter_context(nc.sbuf_tensor("out_sb", [128, 2048], F32))
            pf = fctx.enter_context(nc.psum_tensor("pf", [128, 1024], F32))
            psf = fctx.enter_context(nc.psum_tensor("psf", [128, 1024], F32))
            pb = fctx.enter_context(nc.psum_tensor("pb", [128, 1024], F32))

            b.dma("sync", lambda e: e.dma_start(out=scb_row[:], in_=scb_h), sq)
            b.dma("sync", lambda e: e.dma_start(out=bias_row[:], in_=bias_h), sq)
            b.dma("sync", lambda e: e.dma_start(out=c127[:], in_=c127_h), sq)
            rows_ld = sq.n
            b.wait("tensor", sq, rows_ld)
            for hf in range(2):
                b.op("tensor", lambda e, hf=hf: e.matmul(
                    pb[:, hf * 512:(hf + 1) * 512], c127[:],
                    scb_row[0:1, hf * 512:(hf + 1) * 512], start=True, stop=True), inc=pe)
            bc1 = pe.n
            b.wait("vector", pe, bc1)
            b.op("vector", lambda e: e.tensor_copy(out=scb_bc[:], in_=pb[:]), inc=dve)
            bcast_n = dve.n
            b.wait("scalar", sq, rows_ld)
            b.op("scalar", lambda e: e.activation(
                out=rr1[:], in_=scb_row[:], func=AF.Ln, scale=1.0 / 127.0), inc=act)
            b.wait("scalar", act, act.n)
            b.op("scalar", lambda e: e.activation(
                out=rr2[:], in_=rr1[:], func=AF.Exp, scale=-1.0), inc=act)
            rr_n = act.n
            b.wait("vector", act, rr_n)
            b.op("vector", lambda e: e.tensor_tensor(
                out=browbf[:], in0=bias_row[:], in1=rr2[:], op=ALU.mult), inc=dve)
            brow_n = dve.n

            b.wait("tensor", wq, fw_ld)

            pf_free = [0, 0]
            psf_free = [0, 0]
            t1_free = [[0, 0], [0, 0]]
            g_free = [[0, 0], [0, 0]]
            t3_free = [[0, 0], [0, 0]]
            out_free = [0, 0]
            out_free4 = [0, 0, 0, 0]

            xld_at = [0] * (NT + 2)

            def issue_xload(i):
                xpar = i % 4
                b.wait("sync", st, out_free4[xpar])
                b.dma("sync", lambda e, i=i, xpar=xpar: e.dma_start(
                    out=x_sb[:, xpar * 1024:(xpar + 1) * 1024],
                    in_=x_hbm[i * 128:(i + 1) * 128, :]), xq)
                xld_at[i] = xq.n

            issue_xload(0)
            issue_xload(1)
            for i in range(NT):
                par = i % 2
                if i + 2 < NT:
                    issue_xload(i + 2)
                xld = xld_at[i]
                for nh in range(2):
                    b.wait("tensor", act, t1_free[par][nh])
                    b.wait("tensor", dve, pf_free[nh])
                    for c in range(8):
                        b.op("tensor", lambda e, c=c, i=i, nh=nh: e.matmul(
                            pf[:, nh * 512:(nh + 1) * 512],
                            xnT[:, c * 2048 + i * 128:c * 2048 + i * 128 + 128],
                            cbt_sb[:, c * 1024 + nh * 512:c * 1024 + nh * 512 + 512],
                            start=(c == 0), stop=False))
                    b.wait("tensor", dve, brow_n)
                    b.op("tensor", lambda e, nh=nh: e.matmul(
                        pf[:, nh * 512:(nh + 1) * 512], onesbf[:],
                        browbf[0:1, nh * 512:(nh + 1) * 512],
                        start=False, stop=True), inc=pe)
                    ffn_n = pe.n
                    b.wait("vector", pe, ffn_n)
                    b.wait("vector", dve, bcast_n)
                    b.wait("vector", act, t1_free[par][nh])
                    b.op("vector", lambda e, nh=nh, par=par: e.tensor_tensor(
                        out=t1[:, par * 1024 + nh * 512:par * 1024 + nh * 512 + 512],
                        in0=pf[:, nh * 512:(nh + 1) * 512],
                        in1=scb_bc[:, nh * 512:(nh + 1) * 512], op=ALU.mult), inc=dve)
                    pf_free[nh] = dve.n
                    t1n = dve.n
                    b.wait("scalar", dve, t1n)
                    b.wait("scalar", dve, g_free[par][nh])
                    b.op("scalar", lambda e, nh=nh, par=par: e.activation(
                        out=g_sb[:, par * 1024 + nh * 512:par * 1024 + nh * 512 + 512],
                        in_=t1[:, par * 1024 + nh * 512:par * 1024 + nh * 512 + 512],
                        func=AF.Silu), inc=act)
                    t1_free[par][nh] = act.n
                    gn = act.n
                    b.wait("tensor", dve, psf_free[nh])
                    for f in range(8):
                        b.op("tensor", lambda e, f=f, i=i, nh=nh: e.matmul(
                            psf[:, nh * 512:(nh + 1) * 512],
                            yT[:, f * 2048 + i * 128:f * 2048 + i * 128 + 128],
                            wo_sb[:, f * 1024 + nh * 512:f * 1024 + nh * 512 + 512],
                            start=(f == 0), stop=(f == 7)),
                            inc=(pe if f == 7 else None))
                    wo_n = pe.n
                    b.wait("vector", pe, wo_n)
                    b.wait("vector", act, gn)
                    b.wait("vector", pool, t3_free[par][nh])
                    b.op("vector", lambda e, nh=nh, par=par: e.tensor_tensor(
                        out=t3[:, par * 1024 + nh * 512:par * 1024 + nh * 512 + 512],
                        in0=psf[:, nh * 512:(nh + 1) * 512],
                        in1=g_sb[:, par * 1024 + nh * 512:par * 1024 + nh * 512 + 512],
                        op=ALU.mult), inc=dve)
                    psf_free[nh] = dve.n
                    g_free[par][nh] = dve.n
                    t3n = dve.n
                    b.wait("gpsimd", dve, t3n)
                    b.wait("gpsimd", xq, xld)
                    b.op("gpsimd", lambda e, nh=nh, par=par, xpar=i % 4: e.tensor_tensor(
                        out=out_sb[:, par * 1024 + nh * 512:par * 1024 + nh * 512 + 512],
                        in0=t3[:, par * 1024 + nh * 512:par * 1024 + nh * 512 + 512],
                        in1=x_sb[:, xpar * 1024 + nh * 512:xpar * 1024 + nh * 512 + 512],
                        op=ALU.add), inc=pool)
                    t3_free[par][nh] = pool.n
                outn = pool.n
                b.wait("sync", pool, outn)
                b.dma("sync", lambda e, i=i, par=par: e.dma_start(
                    out=y_hbm[i * 128:(i + 1) * 128, :],
                    in_=out_sb[:, par * 1024:(par + 1) * 1024]), st)
                out_free[par] = st.n
                out_free4[i % 4] = st.n

            b.wait("sync", st, st.n)
            flush()

    return nc


def _rope_scale_np():
    t = np.arange(T, dtype=np.float32)
    j = np.arange(0, DHR, 2, dtype=np.float32) / np.float32(DHR)
    inv = (np.float32(1.0) / np.power(np.float32(10000.0), j)).astype(np.float32)
    freqs = (t[:, None] * inv[None, :]).astype(np.float32)
    emb = np.concatenate([freqs, freqs], axis=-1)
    scale = (np.cos(emb) + np.sin(emb)).astype(np.float32)
    rep = np.ascontiguousarray(np.tile(scale.T, (4, 1)).astype(np.float32))
    return rep


def _masks_np():
    kk = np.arange(128)[:, None]
    qp = np.arange(512)[None, :]
    out = np.zeros((128, 2560), dtype=ml_dtypes.bfloat16)
    for r in range(4):
        out[:, r * 512:(r + 1) * 512] = np.where(
            qp < 128 * r + kk, np.float32(-1e5), np.float32(0)
        ).astype(ml_dtypes.bfloat16)
    return out


def make_inputs(inp):
    core_common = dict(
        W_DQ=np.ascontiguousarray(np.asarray(inp["L0_attn_W_DQ"], dtype=np.float32)),
        W_UQ=np.ascontiguousarray(np.asarray(inp["L0_attn_W_UQ"], dtype=np.float32)),
        W_DKV=np.ascontiguousarray(np.asarray(inp["L0_attn_W_DKV"], dtype=np.float32)),
        W_UK=np.ascontiguousarray(np.asarray(inp["L0_attn_W_UK"], dtype=np.float32)),
        W_UV=np.ascontiguousarray(np.asarray(inp["L0_attn_W_UV"], dtype=np.float32)),
        W_QR=np.ascontiguousarray(np.asarray(inp["L0_attn_W_QR"], dtype=np.float32)),
        W_KR=np.ascontiguousarray(np.asarray(inp["L0_attn_W_KR"], dtype=np.float32)),
        W_O=np.ascontiguousarray(np.asarray(inp["L0_attn_W_O"], dtype=np.float32)),
        CBT=np.ascontiguousarray(np.asarray(inp["L0_ffn_CB"], dtype=np.float32).T),
        SCB=np.ascontiguousarray(np.asarray(inp["L0_ffn_SCB"], dtype=np.float32).reshape(1, C)),
        BIAS=np.ascontiguousarray(np.asarray(inp["L0_ffn_bias"], dtype=np.float32).reshape(1, C)),
        ROPE=_rope_scale_np().astype(ml_dtypes.bfloat16),
        MASKS=_masks_np(),
        C127=np.full((1, 128), 1.0 / 127.0, dtype=np.float32),
        ONES=np.ones((1, 128), dtype=np.float32),
        ONESBF=np.ones((1, 128), dtype=ml_dtypes.bfloat16),
        IDENT=np.eye(128, dtype=ml_dtypes.bfloat16),
    )
    x = np.ascontiguousarray(np.asarray(inp["L0_attn_x"], dtype=np.float32))
    in_maps = []
    for bcore in range(8):
        m = dict(core_common)
        m["x"] = np.ascontiguousarray(x[bcore])
        in_maps.append(m)
    return in_maps


def kernel(**inputs):
    in_maps = make_inputs(inputs)
    nc = build_program()
    res = run_bass_kernel_spmd(nc, in_maps, core_ids=list(range(8)))
    return np.stack([r["y"] for r in res.results], axis=0)


# version 44 (final)
# speedup vs baseline: 1.0794x; 1.0146x over previous
import math
import numpy as np
import ml_dtypes
from contextlib import ExitStack

import concourse.bass as bass
import concourse.mybir as mybir
from concourse.bass_utils import run_bass_kernel_spmd

F32 = mybir.dt.float32
BF16 = mybir.dt.bfloat16
AF = mybir.ActivationFunctionType
ALU = mybir.AluOpType
AX = mybir.AxisListType

T = 2048
C = 1024
H = 16
HS = 64
DHR = 32
DQK = HS + DHR
DC = 4096
DKV = 256
NT = T // 128
INV_SQRT = 1.0 / math.sqrt(DQK)

ENGINES = ("tensor", "vector", "scalar", "gpsimd", "sync")


class Chan:
    def __init__(self, sem):
        self.sem = sem
        self.n = 0


class Builder:

    def __init__(self):
        self.plan = {e: [] for e in ENGINES}
        self.lastw = {e: {} for e in ENGINES}

    def wait(self, eng, ch, val=None):
        v = ch.n if val is None else val
        if v <= 0:
            return
        if self.lastw[eng].get(id(ch), -1) >= v:
            return
        self.lastw[eng][id(ch)] = v
        self.plan[eng].append(lambda e, s=ch.sem, vv=v: e.wait_ge(s, vv))

    def op(self, eng, fn, inc=None, k=1):
        if inc is not None:
            inc.n += k
            self.plan[eng].append(
                lambda e, f=fn, s=inc.sem, kk=k: f(e).then_inc(s, kk))
        else:
            self.plan[eng].append(fn)

    def dma(self, eng, fn, ch):
        self.op(eng, fn, inc=ch, k=16)

    def barrier(self, chans):
        snap = [(ch, ch.n) for ch in chans]
        for eng in ENGINES:
            for ch, v in snap:
                self.wait(eng, ch, v)


def build_program(debug=False, phases=3):
    nc = bass.Bass("TRN2", target_bir_lowering=False)

    x_hbm = nc.dram_tensor("x", [T, C], F32, kind="ExternalInput").ap()
    w_dq = nc.dram_tensor("W_DQ", [C, DC], F32, kind="ExternalInput").ap()
    w_uq = nc.dram_tensor("W_UQ", [DC, C], F32, kind="ExternalInput").ap()
    w_dkv = nc.dram_tensor("W_DKV", [C, DKV], F32, kind="ExternalInput").ap()
    w_uk = nc.dram_tensor("W_UK", [DKV, C], F32, kind="ExternalInput").ap()
    w_uv = nc.dram_tensor("W_UV", [DKV, C], F32, kind="ExternalInput").ap()
    w_qr = nc.dram_tensor("W_QR", [DC, H * DHR], F32, kind="ExternalInput").ap()
    w_kr = nc.dram_tensor("W_KR", [C, DHR], F32, kind="ExternalInput").ap()
    w_o = nc.dram_tensor("W_O", [C, C], F32, kind="ExternalInput").ap()
    cbt = nc.dram_tensor("CBT", [C, C], F32, kind="ExternalInput").ap()
    scb_h = nc.dram_tensor("SCB", [1, C], F32, kind="ExternalInput").ap()
    bias_h = nc.dram_tensor("BIAS", [1, C], F32, kind="ExternalInput").ap()
    scale_h = nc.dram_tensor("ROPE", [128, T], BF16, kind="ExternalInput").ap()
    masks_h = nc.dram_tensor("MASKS", [128, 2560], BF16, kind="ExternalInput").ap()
    c127_h = nc.dram_tensor("C127", [1, 128], F32, kind="ExternalInput").ap()
    ones_h = nc.dram_tensor("ONES", [1, 128], F32, kind="ExternalInput").ap()
    onesbf_h = nc.dram_tensor("ONESBF", [1, 128], BF16, kind="ExternalInput").ap()
    ident_h = nc.dram_tensor("IDENT", [128, 128], BF16, kind="ExternalInput").ap()
    y_hbm = nc.dram_tensor("y", [T, C], F32, kind="ExternalOutput").ap()
    qt_kind = "ExternalOutput" if debug else "Internal"
    qt_hbm = nc.dram_tensor("qt", [H, DQK, T], BF16, kind=qt_kind).ap()
    dbg = {}
    if debug:
        dbg["xnT"] = nc.dram_tensor("d_xnT", [128, 16384], BF16, kind="ExternalOutput").ap()
        dbg["ckvT"] = nc.dram_tensor("d_ckvT", [128, 4096], BF16, kind="ExternalOutput").ap()
        dbg["krT"] = nc.dram_tensor("d_krT", [32, 2048], BF16, kind="ExternalOutput").ap()
        dbg["yT"] = nc.dram_tensor("d_yT", [128, 16384], BF16, kind="ExternalOutput").ap()
        dbg["ktb0"] = nc.dram_tensor("d_ktb0", [96, 2048], BF16, kind="ExternalOutput").ap()
        dbg["wdqT"] = nc.dram_tensor("d_wdqT", [128, 32768], BF16, kind="ExternalOutput").ap()
        dbg["weff"] = nc.dram_tensor("d_weff", [128, 12288], BF16, kind="ExternalOutput").ap()
        dbg["ktb1"] = nc.dram_tensor("d_ktb1", [96, 2048], BF16, kind="ExternalOutput").ap()
        dbg["vb0"] = nc.dram_tensor("d_vb0", [128, 2048], BF16, kind="ExternalOutput").ap()
        dbg["vb1"] = nc.dram_tensor("d_vb1", [128, 2048], BF16, kind="ExternalOutput").ap()

    b = Builder()

    with ExitStack() as ctx:
        def chan(name):
            return Chan(ctx.enter_context(nc.semaphore(name)))
        wq = chan("wq")
        sq = chan("sq")
        xq = chan("xq")
        pq = chan("pq")
        vqA = chan("vqA")
        vqB = chan("vqB")
        vqC = chan("vqC")
        vqD = chan("vqD")
        aq = chan("aq")
        st = chan("st")
        pe = chan("pe")
        act = chan("act")
        dve = chan("dve")
        pool = chan("pool")
        ALLCH = [wq, sq, xq, pq, vqA, vqB, vqC, vqD, aq, st, pe, act, dve, pool]

        block = ctx.enter_context(nc.Block())

        def flush():
            for eng in ENGINES:
                fns = b.plan[eng]
                if not fns:
                    continue
                b.plan[eng] = []

                def section(e, fns=fns):
                    for f in fns:
                        f(e)
                getattr(block, eng)(section)

        xnT = ctx.enter_context(nc.sbuf_tensor("xnT", [128, 16384], BF16))
        ckvT = ctx.enter_context(nc.sbuf_tensor("ckvT", [128, 4096], BF16))
        krT = ctx.enter_context(nc.sbuf_tensor("krT", [32, 2048], BF16))
        rope = ctx.enter_context(nc.sbuf_tensor("rope", [128, 2048], BF16))
        c127 = ctx.enter_context(nc.sbuf_tensor("c127", [1, 128], F32))
        onesbf = ctx.enter_context(nc.sbuf_tensor("onesbf", [1, 128], BF16))
        ident = ctx.enter_context(nc.sbuf_tensor("ident", [128, 128], BF16))
        x_sb = ctx.enter_context(nc.sbuf_tensor("x_sb", [128, 4096], F32))
        ktb = [None, None, None]
        vb = [ctx.enter_context(nc.sbuf_tensor("vb0", [128, 2048], BF16)),
              ctx.enter_context(nc.sbuf_tensor("vb1", [128, 2048], BF16)),
              None]
        vstage = ctx.enter_context(nc.sbuf_tensor("vstage", [128, 2048], BF16))

        xld_at = {}
        for g0 in range(3):
            b.dma("sync", lambda e, g0=g0: e.dma_start(
                out=x_sb[:, g0 * 1024:(g0 + 1) * 1024],
                in_=x_hbm[g0 * 128:(g0 + 1) * 128, :]), xq)
            xld_at[g0] = xq.n

        b.dma("sync", lambda e: e.dma_start(out=c127[:], in_=c127_h), sq)
        b.dma("sync", lambda e: e.dma_start(out=onesbf[:], in_=onesbf_h), sq)
        b.dma("sync", lambda e: e.dma_start(out=ident[:], in_=ident_h), sq)
        ident_sq = sq.n
        b.dma("sync", lambda e: e.dma_start(out=rope[:], in_=scale_h), sq)
        const_sq = sq.n

        kt_rd = [[0] * 4 for _ in range(3)]
        vb_rd = [[[] for _ in range(16)] for _ in range(3)]
        vstage_free = [[], [], [], []]
        uk_ld = [0]
        ckv_rdy = [0]

        def emit_build_tile(p, k, prev_pe, psy, psy_free, wuk, wuv, wready):
            bank = k % 2
            bufA, bufB = (2 * p) % 3, (2 * p + 1) % 3
            b.wait("tensor", wready[0], wready[1])
            b.wait("tensor", dve, ckv_rdy[0])
            b.wait("tensor", pe, prev_pe)
            b.wait("tensor", dve, psy_free[bank])
            if k >= 4:
                t4 = k - 4
                for c in range(2):
                    b.op("tensor", lambda e, bank=bank, c=c, p=p, t4=t4, psy=psy, wuk=wuk: e.matmul(
                        psy(bank),
                        wuk[:, c * 1024 + 128 * p:c * 1024 + 128 * p + 128],
                        ckvT[:, c * 2048 + t4 * 512:c * 2048 + t4 * 512 + 512],
                        start=(c == 0), stop=(c == 1)),
                        inc=(pe if c == 1 else None))
                mmn = pe.n
                b.wait("vector", pe, mmn)
                b.op("vector", lambda e, bank=bank, bufA=bufA, t4=t4, psy=psy: e.tensor_copy(
                    out=ktb[bufA][0:64, t4 * 512:(t4 + 1) * 512],
                    in_=psy(bank)[0:64, :]), inc=dve)
                b.op("vector", lambda e, bank=bank, bufB=bufB, t4=t4, psy=psy: e.tensor_copy(
                    out=ktb[bufB][0:64, t4 * 512:(t4 + 1) * 512],
                    in_=psy(bank)[64:128, :]), inc=dve)
                psy_free[bank] = dve.n
                kt_rd[bufA][t4] = kt_rd[bufB][t4] = dve.n
            else:
                t4 = k
                for c in range(2):
                    b.op("tensor", lambda e, bank=bank, c=c, p=p, t4=t4, psy=psy, wuv=wuv: e.matmul(
                        psy(bank),
                        wuv[:, c * 1024 + 128 * p:c * 1024 + 128 * p + 128],
                        ckvT[:, c * 2048 + t4 * 512:c * 2048 + t4 * 512 + 512],
                        start=(c == 0), stop=(c == 1)),
                        inc=(pe if c == 1 else None))
                mmn = pe.n
                b.wait("vector", pe, mmn)
                for chn, v in vstage_free[t4]:
                    b.wait("vector", chn, v)
                b.op("vector", lambda e, bank=bank, t4=t4, psy=psy: e.tensor_copy(
                    out=vstage[:, t4 * 512:(t4 + 1) * 512],
                    in_=psy(bank)), inc=dve)
                psy_free[bank] = dve.n
                evn = dve.n
                b.wait("sync", dve, evn)
                b.wait("sync", pe, prev_pe)
                vf = []
                for tt in range(4 * t4, 4 * t4 + 4):
                    lo = (tt - 4 * t4) * 128 + t4 * 512
                    b.dma("sync", lambda e, bufA=bufA, tt=tt, lo=lo: e.dma_start(
                        out=vb[bufA][:, tt * 128:tt * 128 + 64],
                        in_=vstage[0:64, lo:lo + 128], transpose=True), vqA)
                    b.dma("sync", lambda e, bufB=bufB, tt=tt, lo=lo: e.dma_start(
                        out=vb[bufB][:, tt * 128:tt * 128 + 64],
                        in_=vstage[64:128, lo:lo + 128], transpose=True), vqB)
                    vb_rd[bufA][tt] = [(vqA, vqA.n)]
                    vb_rd[bufB][tt] = [(vqB, vqB.n)]
                    vf.extend([(vqA, vqA.n), (vqB, vqB.n)])
                vstage_free[t4] = vf

        build_sched = []
        init_settle = []

        with ExitStack() as pctx:
            xn_tok = pctx.enter_context(nc.sbuf_tensor("xn_tok", [128, 2048], BF16))
            sq_scr = pctx.enter_context(nc.sbuf_tensor("sq_scr", [128, 1024], F32))
            stats = pctx.enter_context(nc.sbuf_tensor("stats", [128, 32], F32))
            ptr = pctx.enter_context(nc.psum_tensor("ptr", [128, 2048], BF16))
            wdq_b = pctx.enter_context(nc.sbuf_tensor("wdq_b", [128, 2048], BF16))
            wdqT = pctx.enter_context(nc.sbuf_tensor("wdqT", [128, 32768], BF16))
            wu_mt = pctx.enter_context(nc.sbuf_tensor("wu_mt", [128, 12288], BF16))
            weff_sb = pctx.enter_context(nc.sbuf_tensor("weff_sb", [128, 12288], BF16))
            wdkv_b = pctx.enter_context(nc.sbuf_tensor("wdkv_b", [128, 2048], BF16))
            wkr_b = pctx.enter_context(nc.sbuf_tensor("wkr_b", [128, 256], BF16))
            stage = pctx.enter_context(nc.sbuf_tensor("stage", [128, 1024], BF16))
            ps_main = pctx.enter_context(nc.psum_tensor("ps_main", [128, 2048], F32))
            ps_init = pctx.enter_context(nc.psum_tensor("ps_init", [128, 1024], F32))

            def psy_p(bank):
                return ps_init[:, bank * 512:(bank + 1) * 512]
            psy_free_p = [0, 0]
            cast_slack = [0, 0]
            wdkv_reuse = [0]
            wkr_reuse = [0]

            pp_free = [(dve, 0)] * 4
            x_free = [0, 0, 0, 0]
            x_free_act = [0, 0, 0, 0]
            xn_free_pe = [0, 0]
            stats_free = [0, 0, 0, 0]
            ptr_free = [0, 0]
            sqscr_free = [0, 0]
            stage_free = [0, 0]
            cq_slot = [0]
            stg = [0]
            wdq_use = [0, 0]
            wuq_use = [0, 0]

            def pslot():
                s = cq_slot[0] % 4
                cq_slot[0] += 1
                return s

            def stage_slot():
                return 0

            def issue_xload_p(gt):
                par = gt % 4
                b.wait("sync", dve, x_free[par])
                b.wait("sync", act, x_free_act[par])
                b.dma("sync", lambda e, gt=gt, par=par: e.dma_start(
                    out=x_sb[:, par * 1024:(par + 1) * 1024],
                    in_=x_hbm[gt * 128:(gt + 1) * 128, :]), xq)
                xld_at[gt] = xq.n

            def issue_xdummy_p():
                b.wait("sync", dve, x_free[0])
                b.wait("sync", act, x_free_act[0])
                b.dma("sync", lambda e: e.dma_start(
                    out=x_sb[:, 0:1024], in_=x_hbm[0:128, :]), xq)
                xld_at[16] = xq.n

            xnT_rth = {0: [None, None], 1: [None, None]}
            xnT_rdy = {}
            st1 = {}

            if True:
                def ln_stage1(tt, tc):
                    gt = tc * 8 + tt
                    par = tt % 4
                    so = par * 8
                    nxt = gt + 3
                    if nxt <= 15:
                        issue_xload_p(nxt)
                    elif nxt == 16:
                        issue_xdummy_p()
                    ld = xld_at.get(gt + 2)
                    if ld is None:
                        ld = xld_at[gt + 1]
                    b.wait("vector", sq, const_sq)
                    b.wait("scalar", sq, const_sq)
                    b.wait("vector", xq, ld)
                    b.wait("vector", dve, stats_free[par])
                    b.op("vector", lambda e, par=par, so=so: e.reduce_sum(
                        out=stats[:, so:so + 1],
                        in_=x_sb[:, par * 1024:(par + 1) * 1024], axis=AX.X), inc=dve)
                    s1_n = dve.n
                    b.wait("scalar", xq, ld)
                    b.wait("scalar", dve, sqscr_free[0])
                    b.op("scalar", lambda e, par=par: e.activation(
                        out=sq_scr[:],
                        in_=x_sb[:, par * 1024:(par + 1) * 1024],
                        func=AF.Square), inc=act)
                    x_free_act[par] = act.n
                    sq_n = act.n
                    b.wait("vector", act, sq_n)
                    b.op("vector", lambda e, so=so: e.reduce_sum(
                        out=stats[:, so + 1:so + 2],
                        in_=sq_scr[:],
                        axis=AX.X), inc=dve)
                    sqscr_free[0] = dve.n
                    ss_n = dve.n
                    b.wait("scalar", dve, s1_n)
                    b.op("scalar", lambda e, so=so: e.activation(
                        out=stats[:, so + 2:so + 3], in_=stats[:, so:so + 1],
                        func=AF.Copy, scale=-1.0 / 1024.0), inc=act)
                    st1[tt] = (par, so, ss_n, act.n)

                def ln_stage2(tt, tc):
                    par, so, ss_n, nm_n = st1.pop(tt)
                    xpar = tt % 2
                    b.wait("vector", act, nm_n)
                    b.op("vector", lambda e, so=so: e.tensor_scalar(
                        out=stats[:, so + 4:so + 5], in0=stats[:, so + 2:so + 3],
                        scalar1=stats[:, so + 2:so + 3], scalar2=1e-5,
                        op0=ALU.mult, op1=ALU.subtract), inc=dve)
                    m2_n = dve.n
                    b.wait("vector", dve, max(ss_n, m2_n))
                    b.op("vector", lambda e, so=so: e.tensor_scalar(
                        out=stats[:, so + 5:so + 6], in0=stats[:, so + 1:so + 2],
                        scalar1=1.0 / 1024.0, scalar2=stats[:, so + 4:so + 5],
                        op0=ALU.mult, op1=ALU.subtract), inc=dve)
                    var_n = dve.n
                    b.wait("scalar", dve, var_n)
                    b.op("scalar", lambda e, so=so: e.activation(
                        out=stats[:, so + 6:so + 7], in_=stats[:, so + 5:so + 6],
                        func=AF.Ln), inc=act)
                    b.wait("scalar", act, act.n)
                    b.op("scalar", lambda e, so=so: e.activation(
                        out=stats[:, so + 3:so + 4], in_=stats[:, so + 6:so + 7],
                        func=AF.Exp, scale=-0.5), inc=act)
                    rstd_n = act.n
                    b.wait("vector", act, rstd_n)
                    b.wait("vector", pe, xn_free_pe[xpar])
                    b.op("vector", lambda e, par=par, so=so, xpar=xpar: e.tensor_scalar(
                        out=xn_tok[:, xpar * 1024:(xpar + 1) * 1024],
                        in0=x_sb[:, par * 1024:(par + 1) * 1024],
                        scalar1=stats[:, so + 2:so + 3], scalar2=stats[:, so + 3:so + 4],
                        op0=ALU.add, op1=ALU.mult), inc=dve)
                    x_free[par] = dve.n
                    xn_n = dve.n
                    stats_free[par] = dve.n
                    b.wait("tensor", sq, const_sq)
                    b.wait("tensor", dve, xn_n)
                    for c in range(8):
                        ps = c % 2
                        b.wait("tensor", dve, ptr_free[ps])
                        b.op("tensor", lambda e, c=c, ps=ps, xpar=xpar: e.transpose(
                            ptr[:, ps * 1024:ps * 1024 + 128],
                            xn_tok[:, xpar * 1024 + c * 128:xpar * 1024 + c * 128 + 128],
                            ident[:]), inc=pe)
                        trn = pe.n
                        b.wait("vector", pe, trn)
                        b.op("vector", lambda e, c=c, ps=ps, tc=tc, tt=tt: e.tensor_copy(
                            out=xnT[:, c * 2048 + tc * 1024 + tt * 128:
                                    c * 2048 + tc * 1024 + tt * 128 + 128],
                            in_=ptr[:, ps * 1024:ps * 1024 + 128]), inc=dve)
                        ptr_free[ps] = dve.n
                    xn_free_pe[xpar] = pe.n
                    if tt == 3:
                        xnT_rth[tc][0] = dve.n
                    if tt == 7:
                        xnT_rth[tc][1] = dve.n
                        xnT_rdy[tc] = dve.n

                def ln_piece(k, tc):
                    if k >= 1:
                        ln_stage2(k - 1, tc)
                    if k < 8:
                        ln_stage1(k, tc)

            def emit_ckv_kr(tc):
                if tc == 1:
                    b.wait("gpsimd", pe, wdkv_reuse[0])
                b.dma("gpsimd", lambda e: e.dma_start(
                    out=wdkv_b.rearrange("p (c m) -> p c m", m=256),
                    in_=w_dkv.rearrange("(c p) m -> p c m", p=128)), wq)
                ld = wq.n
                for m in range(2):
                    for th in range(2):
                        s = pslot()
                        b.wait("tensor", wq, ld)
                        b.wait("tensor", dve, xnT_rdy[tc])
                        b.wait("tensor", *pp_free[s])
                        for c in range(8):
                            b.op("tensor", lambda e, s=s, c=c, m=m, tc=tc, th=th: e.matmul(
                                ps_main[:, s * 512:(s + 1) * 512],
                                wdkv_b[:, c * 256 + m * 128:c * 256 + m * 128 + 128],
                                xnT[:, c * 2048 + tc * 1024 + th * 512:
                                    c * 2048 + tc * 1024 + th * 512 + 512],
                                start=(c == 0), stop=(c == 7)),
                                inc=(pe if c == 7 else None))
                        mmn = pe.n
                        b.wait("vector", pe, mmn)
                        b.op("vector", lambda e, s=s, m=m, tc=tc, th=th: e.tensor_copy(
                            out=ckvT[:, m * 2048 + tc * 1024 + th * 512:
                                     m * 2048 + tc * 1024 + th * 512 + 512],
                            in_=ps_main[:, s * 512:(s + 1) * 512]), inc=dve)
                        pp_free[s] = (dve, dve.n)
                wdkv_reuse[0] = pe.n
                ckv_rdy[0] = dve.n

                if tc == 1:
                    b.wait("gpsimd", pe, wkr_reuse[0])
                b.dma("gpsimd", lambda e: e.dma_start(
                    out=wkr_b.rearrange("p (c m) -> p c m", m=32),
                    in_=w_kr.rearrange("(c p) m -> p c m", p=128)), wq)
                ld = wq.n
                for th in range(2):
                    s = pslot()
                    b.wait("tensor", wq, ld)
                    b.wait("tensor", *pp_free[s])
                    for c in range(8):
                        b.op("tensor", lambda e, s=s, c=c, tc=tc, th=th: e.matmul(
                            ps_main[0:32, s * 512:(s + 1) * 512],
                            wkr_b[:, c * 32:c * 32 + 32],
                            xnT[:, c * 2048 + tc * 1024 + th * 512:
                                c * 2048 + tc * 1024 + th * 512 + 512],
                            start=(c == 0), stop=(c == 7)),
                            inc=(pe if c == 7 else None))
                    mmn = pe.n
                    off = tc * 1024 + th * 512
                    b.wait("vector", pe, mmn)
                    b.wait("vector", sq, const_sq)
                    b.op("vector", lambda e, s=s, off=off: e.tensor_tensor(
                        out=krT[0:32, off:off + 512],
                        in0=ps_main[0:32, s * 512:(s + 1) * 512],
                        in1=rope[0:32, off:off + 512], op=ALU.mult), inc=dve)
                    pp_free[s] = (dve, dve.n)
                wkr_reuse[0] = pe.n

            for i3 in range(2):
                b.op("gpsimd", lambda e, i3=i3: e.memset(vb[i3][:], 1.0),
                     inc=pool)
            vb_init_n = pool.n

            wdq_ld = [0] * 32
            wu_ld = {}
            wu_war = [0] * 32
            wdq_war = [0, 0]
            wdqT_rd = [0] * 32
            weff_mm = [[0] * 4 for _ in range(2)]

            def emit_wu_load(mt, d):
                b.wait("gpsimd", pe, wu_war[d])
                lo, hi = mt * 384, mt * 384 + 384
                spans = []
                if lo < 1024:
                    spans.append((w_uq, lo, min(hi, 1024), 0))
                if hi > 1024:
                    spans.append((w_qr, max(lo, 1024) - 1024, hi - 1024,
                                  max(lo, 1024) - lo))
                for srcw, c0s, c1s, off in spans:
                    b.dma("gpsimd", lambda e, srcw=srcw, c0s=c0s, c1s=c1s, off=off, d=d: e.dma_start(
                        out=wu_mt[:, d * 384 + off:d * 384 + off + (c1s - c0s)],
                        in_=srcw[d * 128:(d + 1) * 128, c0s:c1s]), wq)
                wu_ld[(mt, d)] = wq.n
                if d == 31:
                    for _ in range(2):
                        b.dma("gpsimd", lambda e: e.dma_start(
                            out=onesbf[:], in_=onesbf_h), wq)

            def emit_weff_mms(mt, cbg, d):
                b.wait("tensor", wq, wu_ld[(mt, d)] + 32)
                b.wait("tensor", dve, wdqT_rd[d])
                for cb4 in range(4):
                    cb = cbg * 4 + cb4
                    if d == 0:
                        b.wait("tensor", *pp_free[cb4])
                    b.op("tensor", lambda e, cb=cb, cb4=cb4, d=d: e.matmul(
                        ps_main[:, cb4 * 512:cb4 * 512 + 384],
                        wdqT[:, d * 1024 + cb * 128:d * 1024 + cb * 128 + 128],
                        wu_mt[:, d * 384:(d + 1) * 384],
                        start=(d == 0), stop=(d == 31)),
                        inc=(pe if (d == 31 or (cbg == 1 and cb4 == 3)) else None))
                    if d == 31:
                        weff_mm[cbg][cb4] = pe.n

            def emit_weff_evac(mt, cbg):
                for cb4 in range(4):
                    cb = cbg * 4 + cb4
                    b.wait("vector", pe, weff_mm[cbg][cb4])
                    b.op("vector", lambda e, mt=mt, cb=cb, cb4=cb4: e.tensor_copy(
                        out=weff_sb[:, cb * 1536 + mt * 384:cb * 1536 + (mt + 1) * 384],
                        in_=ps_main[:, cb4 * 512:cb4 * 512 + 384]), inc=dve)
                    pp_free[cb4] = (dve, dve.n)

            for tc9 in range(2):
                for k9 in range(9):
                    build_sched.append(lambda k9=k9, tc9=tc9: ln_piece(k9, tc9))

            for i in range(32):
                par = i % 2
                b.wait("gpsimd", pe, wdq_war[par])
                b.dma("gpsimd", lambda e, i=i, par=par: e.dma_start(
                    out=wdq_b[:, par * 1024:(par + 1) * 1024]
                        .rearrange("p (c d) -> p c d", d=128),
                    in_=w_dq[:, i * 128:(i + 1) * 128]
                        .rearrange("(c p) d -> p c d", p=128)), wq)
                wdq_ld[i] = wq.n
                emit_wu_load(0, i)
                ps = i % 2
                b.wait("tensor", wq, wu_ld[(0, i)] + (16 if i == 31 else 32))
                b.wait("tensor", dve, ptr_free[ps])
                b.wait("tensor", sq, ident_sq)
                for cb in range(8):
                    b.op("tensor", lambda e, cb=cb, ps=ps, par=par: e.transpose(
                        ptr[:, ps * 1024 + cb * 128:ps * 1024 + cb * 128 + 128],
                        wdq_b[:, par * 1024 + cb * 128:par * 1024 + cb * 128 + 128],
                        ident[:]), inc=(pe if cb == 7 else None))
                wdq_war[par] = pe.n
                b.wait("vector", pe, pe.n)
                b.op("vector", lambda e, i=i, ps=ps: e.tensor_copy(
                    out=wdqT[:, i * 1024:(i + 1) * 1024],
                    in_=ptr[:, ps * 1024:(ps + 1) * 1024]), inc=dve)
                wdqT_rd[i] = dve.n
                ptr_free[ps] = dve.n
                emit_weff_mms(0, 0, i)
                if build_sched:
                    build_sched.pop(0)()
            emit_weff_evac(0, 0)
            emit_ckv_kr(0)

            for mt in range(4):
                for cbg in range(2):
                    if mt == 0 and cbg == 0:
                        continue
                    for d in range(32):
                        emit_weff_mms(mt, cbg, d)
                        if cbg == 1:
                            wu_war[d] = pe.n
                            if mt < 3:
                                emit_wu_load(mt + 1, d)
                    emit_weff_evac(mt, cbg)
            weff_rdy = dve.n

            emit_ckv_kr(1)

            b.wait("scalar", dve, max(x_free))
            b.wait("scalar", xq, xld_at[16])
            for src, lo in ((w_uk, 0), (w_uk, 1), (w_uv, 2), (w_uv, 3)):
                b.dma("scalar", lambda e, src=src, lo=lo: e.dma_start(
                    out=x_sb[:, lo * 1024:(lo + 1) * 1024],
                    in_=src[(lo % 2) * 128:(lo % 2) * 128 + 128, :]), aq)
                if lo == 2:
                    cast_slack[0] = aq.n
            b.dma("scalar", lambda e: e.dma_start(out=onesbf[:], in_=onesbf_h), aq)
            cast_slack[1] = aq.n
            b.wait("scalar", pe, pe.n)
            b.wait("scalar", aq, cast_slack[1])
            for hh in range(2):
                b.op("scalar", lambda e, hh=hh: e.activation(
                    out=wdqT[:, hh * 1024:(hh + 1) * 1024],
                    in_=x_sb[:, 2048 + hh * 1024:2048 + (hh + 1) * 1024],
                    func=AF.Copy), inc=act)
            uk_ld[0] = act.n
            build_sched.append(lambda: b.wait("sync", pool, vb_init_n))
            for kk in range(4):
                build_sched.append(lambda kk=kk: emit_build_tile(
                    0, kk, 0, psy_p, psy_free_p,
                    wdqT[:, 0:2048], wdqT[:, 0:2048],
                    (act, uk_ld[0])))

            qstg = [(stage, 0), (xn_tok, 0)]
            qstg_free = [0, 0]
            for tc in range(2):
                boff = tc * 1024
                for j in range(12):
                    g = tc * 12 + j
                    sl = g % 2
                    qt_, qo_ = qstg[sl]
                    gslots = []
                    for th in range(2):
                        s = pslot()
                        gslots.append(s)
                        b.wait("tensor", dve, weff_rdy)
                        b.wait("tensor", *pp_free[s])
                        b.wait("tensor", dve, xnT_rdy[tc])
                        for c in range(8):
                            b.op("tensor", lambda e, s=s, c=c, j=j, tc=tc, th=th: e.matmul(
                                ps_main[:, s * 512:(s + 1) * 512],
                                weff_sb[:, c * 1536 + j * 128:c * 1536 + j * 128 + 128],
                                xnT[:, c * 2048 + tc * 1024 + th * 512:
                                    c * 2048 + tc * 1024 + th * 512 + 512],
                                start=(c == 0), stop=(c == 7)),
                                inc=(pe if c == 7 else None))
                        mmn = pe.n
                        off = tc * 1024 + th * 512
                        if j < 8:
                            b.wait("scalar", pe, mmn)
                            b.wait("scalar", aq, qstg_free[sl])
                            b.op("scalar", lambda e, s=s, th=th, qt_=qt_, qo_=qo_: e.activation(
                                out=qt_[:, qo_ + th * 512:qo_ + th * 512 + 512],
                                in_=ps_main[:, s * 512:(s + 1) * 512],
                                func=AF.Copy), inc=act)
                        else:
                            b.wait("vector", pe, mmn)
                            b.wait("vector", aq, qstg_free[sl])
                            b.wait("vector", sq, const_sq)
                            b.op("vector", lambda e, s=s, th=th, off=off, qt_=qt_, qo_=qo_: e.tensor_tensor(
                                out=qt_[:, qo_ + th * 512:qo_ + th * 512 + 512],
                                in0=ps_main[:, s * 512:(s + 1) * 512],
                                in1=rope[:, off:off + 512],
                                op=ALU.mult), inc=dve)
                            pp_free[s] = (dve, dve.n)
                    if j < 8:
                        b.wait("scalar", act, act.n)
                        b.dma("scalar", lambda e, j=j, boff=boff, qt_=qt_, qo_=qo_: e.dma_start(
                            out=qt_hbm[2 * j:2 * j + 2, 0:64, boff:boff + 1024],
                            in_=qt_[:, qo_:qo_ + 1024]), aq)
                        for s2 in gslots:
                            pp_free[s2] = (aq, aq.n)
                    else:
                        cpn = dve.n
                        jj = j - 8
                        b.wait("scalar", dve, cpn)
                        b.dma("scalar", lambda e, jj=jj, boff=boff, qt_=qt_, qo_=qo_: e.dma_start(
                            out=qt_hbm[4 * jj:4 * jj + 4, 64:96, boff:boff + 1024],
                            in_=qt_[:, qo_:qo_ + 1024]), aq)
                    qstg_free[sl] = aq.n
                    if build_sched:
                        build_sched.pop(0)()

            assert not build_sched
            init_settle.extend([(vqA, vqA.n), (vqB, vqB.n), (dve, dve.n)])

            if debug:
                b.wait("sync", dve, dve.n)
                b.wait("sync", act, act.n)
                b.dma("sync", lambda e: e.dma_start(out=dbg["weff"], in_=weff_sb[:]), sq)
                b.dma("sync", lambda e: e.dma_start(out=dbg["xnT"], in_=xnT[:]), sq)
                b.dma("sync", lambda e: e.dma_start(out=dbg["ckvT"], in_=ckvT[:]), sq)
                b.dma("sync", lambda e: e.dma_start(out=dbg["krT"], in_=krT[:]), sq)
            b.barrier(ALLCH)
            flush()

        if phases < 2:
            return nc

        yT = ctx.enter_context(nc.sbuf_tensor("yT", [128, 16384], BF16))
        wo_sb = ctx.enter_context(nc.sbuf_tensor("wo_sb", [128, 8192], BF16))
        cbt_sb = ctx.enter_context(nc.sbuf_tensor("cbt_sb", [128, 8192], BF16))
        with ExitStack() as actx:
            wuk_sb = actx.enter_context(nc.sbuf_tensor("wuk_sb", [128, 2048], BF16))
            wuv_sb = actx.enter_context(nc.sbuf_tensor("wuv_sb", [128, 2048], BF16))
            ktb[0] = actx.enter_context(nc.sbuf_tensor("ktb0", [96, 2048], BF16))
            ktb[1] = actx.enter_context(nc.sbuf_tensor("ktb1", [96, 2048], BF16))
            ktb[2] = actx.enter_context(nc.sbuf_tensor("ktb2", [96, 2048], BF16))
            vb[2] = actx.enter_context(nc.sbuf_tensor("vb2", [128, 2048], BF16))
            qtb = [actx.enter_context(nc.sbuf_tensor(f"qtb{i}", [96, 2048], BF16))
                   for i in range(2)]
            masks = actx.enter_context(nc.sbuf_tensor("masks", [128, 2560], BF16))
            pt = actx.enter_context(nc.sbuf_tensor("pt", [128, 2048], BF16))
            ln_tmp = actx.enter_context(nc.sbuf_tensor("ln_tmp", [64, 1024], F32))
            rb = actx.enter_context(nc.sbuf_tensor("rb", [64, 1024], F32))
            pss_all = actx.enter_context(nc.psum_tensor("pss_all", [128, 2048], F32))

            def pss_v(s, bank):
                return pss_all[:, (2 * bank + s) * 512:(2 * bank + s + 1) * 512]
            ps_yp = [actx.enter_context(nc.psum_tensor(f"ps_yp{q}", [128, 1024], F32))
                     for q in range(2)]

            def psy(g):
                return ps_yp[g // 2][:, (g % 2) * 512:(g % 2 + 1) * 512]

            b.dma("gpsimd", lambda e: e.dma_start(
                out=wuv_sb.rearrange("p (c m) -> p c m", m=1024),
                in_=w_uv.rearrange("(c p) m -> p c m", p=128)), wq)
            uv_ld_a = wq.n
            b.dma("gpsimd", lambda e: e.dma_start(
                out=wo_sb.rearrange("p (f m) -> p f m", m=1024),
                in_=w_o.rearrange("(f p) m -> p f m", p=128)), wq)
            b.dma("gpsimd", lambda e: e.dma_start(
                out=cbt_sb.rearrange("p (c m) -> p c m", m=1024),
                in_=cbt.rearrange("(c p) m -> p c m", p=128)), wq)
            fw_ld = wq.n

            for hh in range(2):
                b.op("scalar", lambda e, hh=hh: e.activation(
                    out=wuk_sb[:, hh * 1024:(hh + 1) * 1024],
                    in_=x_sb[:, hh * 1024:(hh + 1) * 1024],
                    func=AF.Copy), inc=act)
            uk_cast = act.n

            b.op("gpsimd", lambda e: e.memset(vb[2][:], 1.0), inc=pool)
            amem = pool.n
            for i3 in range(3):
                b.op("vector", lambda e, i3=i3: e.tensor_copy(
                    out=ktb[i3][64:96, :], in_=krT[:]), inc=dve)
            ainit = dve.n

            b.dma("sync", lambda e: e.dma_start(out=masks[:], in_=masks_h), sq)
            masks_ld = sq.n
            b.wait("tensor", sq, masks_ld)
            b.dma("scalar", lambda e: e.dma_start(out=qtb[0][:], in_=qt_hbm[0]), pq)
            qt_ld = [pq.n, 0]
            b.dma("scalar", lambda e: e.dma_start(out=qtb[1][:], in_=qt_hbm[1]), pq)
            qt_ld[1] = pq.n

            war_head = [0] * 17
            psy_free = [0, 0, 0, 0]
            exp_bank = [[0, 0], [0, 0]]
            rb_free = [0]

            STREAM = [[(0, j) for j in range(4)] + [(3, j) for j in range(16)],
                      [(1, j) for j in range(8)] + [(2, j) for j in range(12)]]

            pt_pv = [[0, 0], [0, 0]]
            fin_ready = [0, 0]

            def emit_scores(h, s, g, j, par, bank):
                boundary = (g == j // 4)
                c0 = max(0, j * 128 - 512 * g)
                b.wait("tensor", dve, kt_rd[par][j // 4])
                b.op("tensor", lambda e, s=s, g=g, j=j, par=par, bank=bank, c0=c0: e.matmul(
                    pss_v(s, bank)[:, c0:512],
                    ktb[par][0:96, j * 128:j * 128 + 128],
                    qtb[h % 2][0:96, g * 512 + c0:(g + 1) * 512],
                    start=True, stop=not boundary),
                    inc=(None if boundary else pe))
                if boundary:
                    r = j % 4
                    b.op("tensor", lambda e, s=s, bank=bank, r=r, c0=c0: e.matmul(
                        pss_v(s, bank)[:, c0:c0 + 128],
                        ident[:], masks[:, r * 512 + c0:r * 512 + c0 + 128],
                        start=False, stop=True), inc=pe)
                return c0

            def emit_exp2(bank, c0A):
                smm = pe.n
                b.wait("scalar", pe, smm)
                b.wait("scalar", pe, max(pt_pv[0][bank], pt_pv[1][bank]))
                lo = 2 * bank * 512 + c0A
                hi = (2 * bank + 2) * 512
                b.op("scalar", lambda e, lo=lo, hi=hi: e.activation(
                    out=pt[:, lo:hi], in_=pss_all[:, lo:hi],
                    func=AF.Exp, scale=INV_SQRT), inc=act)
                exp_bank[0][bank] = exp_bank[1][bank] = act.n
                return act.n

            def emit_pv(h, s, g, j, par, hq, bank, expn):
                reg = (2 * bank + s) * 512
                c0 = max(0, j * 128 - 512 * g)
                for chn, v in vb_rd[par][j]:
                    b.wait("tensor", chn, v)
                if j == 0:
                    b.wait("tensor", dve, psy_free[g])
                b.op("tensor", lambda e, g=g, j=j, par=par, reg=reg, c0=c0: e.matmul(
                    psy(g)[:, c0:512],
                    vb[par][:, j * 128:(j + 1) * 128],
                    pt[:, reg + c0:reg + 512],
                    start=(j == 0), stop=(j == 4 * g + 3)), inc=pe)
                pt_pv[s][bank] = pe.n
                if j == 4 * g + 3 and (g % 2 == 1 or g == 2):
                    fin_ready[g // 2] += 1
                    if g // 2 == 0 or fin_ready[1] == 2:
                        emit_finish(h, g // 2, hq)

            finish_q = []

            def emit_finish(h, q, hq):
                psn = pe.n

                def p_ln(q=q, psn=psn):
                    b.wait("scalar", pe, psn)
                    b.wait("scalar", dve, rb_free[0])
                    b.op("scalar", lambda e, q=q: e.activation(
                        out=ln_tmp[:],
                        in_=ps_yp[q][64:128, :],
                        func=AF.Ln), inc=act)

                def p_exp():
                    b.wait("scalar", act, act.n)
                    b.op("scalar", lambda e: e.activation(
                        out=rb[:], in_=ln_tmp[:],
                        func=AF.Exp, scale=-1.0), inc=act)

                def p_mult(q=q, h=h, hq=hq):
                    b.wait("vector", act, act.n)
                    b.op("vector", lambda e, q=q, h=h, hq=hq: e.tensor_tensor(
                        out=yT[64 * (h % 2):64 * (h % 2) + 64,
                               hq + q * 1024:hq + (q + 1) * 1024],
                        in0=ps_yp[q][0:64, :], in1=rb[:], op=ALU.mult), inc=dve)
                    psy_free[2 * q] = psy_free[2 * q + 1] = dve.n
                    rb_free[0] = dve.n

                finish_q.extend([p_ln, p_exp, p_mult])

            for kk in range(4, 8):
                emit_build_tile(0, kk, 0, psy, psy_free,
                                wuk_sb[:], wuk_sb[:], (act, uk_cast))
            if debug:
                for chn, v in init_settle:
                    b.wait("sync", chn, v)
                b.dma("sync", lambda e: e.dma_start(out=dbg["ktb0"], in_=ktb[0][:]), sq)
                b.dma("sync", lambda e: e.dma_start(out=dbg["ktb1"], in_=ktb[1][:]), sq)
                b.dma("sync", lambda e: e.dma_start(out=dbg["vb0"], in_=vb[0][:]), sq)
                b.dma("sync", lambda e: e.dma_start(out=dbg["vb1"], in_=vb[1][:]), sq)
                for eng2 in ("tensor", "vector", "scalar"):
                    b.wait(eng2, sq, sq.n)

            for h in range(16):
                par = h % 3
                hq = (h // 2) * 2048
                b.wait("tensor", pq, qt_ld[h % 2])
                b.wait("tensor", dve, ainit)
                b.wait("tensor", pool, amem)
                if h < 2:
                    for chn, v in init_settle:
                        b.wait("tensor", chn, v)
                b.wait("vector", sq, const_sq)
                build_k = [0]
                fin_ready[0] = fin_ready[1] = 0
                pend = []
                for i in range(22):
                    for _ in range(2):
                        if finish_q:
                            finish_q.pop(0)()
                    if len(pend) == 2 or (i >= 20 and pend):
                        (pgA, pjA, pgB, pjB, pexp, pbank) = pend.pop(0)
                        b.wait("tensor", act, pexp)
                        emit_pv(h, 0, pgA, pjA, par, hq, pbank, pexp)
                        emit_pv(h, 1, pgB, pjB, par, hq, pbank, pexp)
                    if i < 20:
                        if i < 2:
                            b.wait("tensor", act, exp_bank[0][i % 2])
                        gA, jA = STREAM[0][i]
                        c0A = emit_scores(h, 0, gA, jA, par, i % 2)
                        gB, jB = STREAM[1][i]
                        emit_scores(h, 1, gB, jB, par, i % 2)
                        en = emit_exp2(i % 2, c0A)
                        pend.append((gA, jA, gB, jB, en, i % 2))
                    if i == 8 and h < 15:
                        npar2 = (h + 1) % 2
                        b.wait("scalar", pe, war_head[h - 1] if h >= 1 else 0)
                        b.dma("scalar", lambda e, h=h, npar2=npar2: e.dma_start(
                            out=qtb[npar2][:], in_=qt_hbm[h + 1]), pq)
                        qt_ld[npar2] = pq.n
                    if i >= 11 and (h % 2 == 1) and h < 15 and build_k[0] < 8:
                        kk2 = build_k[0]
                        if kk2 == 0:
                            b.wait("sync", pool, amem)
                        emit_build_tile((h + 1) // 2, kk2,
                                        war_head[h - 1] if h >= 1 else 0,
                                        psy, psy_free, wuk_sb[:], wuv_sb[:],
                                        (wq, uv_ld_a) if kk2 < 4 else
                                        (act, uk_cast))
                        build_k[0] += 1
                war_head[h] = pe.n
            while finish_q:
                finish_q.pop(0)()

            if debug:
                b.wait("sync", dve, dve.n)
                b.dma("sync", lambda e: e.dma_start(out=dbg["yT"], in_=yT[:]), sq)
            b.barrier(ALLCH)
            flush()

        if phases < 3:
            return nc

        with ExitStack() as fctx:
            scb_bc = fctx.enter_context(nc.sbuf_tensor("scb_bc", [128, 1024], F32))
            scb_row = fctx.enter_context(nc.sbuf_tensor("scb_row", [1, 1024], F32))
            bias_row = fctx.enter_context(nc.sbuf_tensor("bias_row", [1, 1024], F32))
            rr1 = fctx.enter_context(nc.sbuf_tensor("rr1", [1, 1024], F32))
            rr2 = fctx.enter_context(nc.sbuf_tensor("rr2", [1, 1024], F32))
            browbf = fctx.enter_context(nc.sbuf_tensor("browbf", [1, 1024], BF16))
            t1 = fctx.enter_context(nc.sbuf_tensor("t1", [128, 2048], F32))
            g_sb = fctx.enter_context(nc.sbuf_tensor("g_sb", [128, 2048], F32))
            t3 = fctx.enter_context(nc.sbuf_tensor("t3", [128, 2048], F32))
            out_sb = fctx.enter_context(nc.sbuf_tensor("out_sb", [128, 2048], F32))
            pf = fctx.enter_context(nc.psum_tensor("pf", [128, 1024], F32))
            psf = fctx.enter_context(nc.psum_tensor("psf", [128, 1024], F32))
            pb = fctx.enter_context(nc.psum_tensor("pb", [128, 1024], F32))

            b.dma("sync", lambda e: e.dma_start(out=scb_row[:], in_=scb_h), sq)
            b.dma("sync", lambda e: e.dma_start(out=bias_row[:], in_=bias_h), sq)
            b.dma("sync", lambda e: e.dma_start(out=c127[:], in_=c127_h), sq)
            rows_ld = sq.n
            b.wait("tensor", sq, rows_ld)
            for hf in range(2):
                b.op("tensor", lambda e, hf=hf: e.matmul(
                    pb[:, hf * 512:(hf + 1) * 512], c127[:],
                    scb_row[0:1, hf * 512:(hf + 1) * 512], start=True, stop=True), inc=pe)
            bc1 = pe.n
            b.wait("vector", pe, bc1)
            b.op("vector", lambda e: e.tensor_copy(out=scb_bc[:], in_=pb[:]), inc=dve)
            bcast_n = dve.n
            b.wait("scalar", sq, rows_ld)
            b.op("scalar", lambda e: e.activation(
                out=rr1[:], in_=scb_row[:], func=AF.Ln, scale=1.0 / 127.0), inc=act)
            b.wait("scalar", act, act.n)
            b.op("scalar", lambda e: e.activation(
                out=rr2[:], in_=rr1[:], func=AF.Exp, scale=-1.0), inc=act)
            rr_n = act.n
            b.wait("vector", act, rr_n)
            b.op("vector", lambda e: e.tensor_tensor(
                out=browbf[:], in0=bias_row[:], in1=rr2[:], op=ALU.mult), inc=dve)
            brow_n = dve.n

            b.wait("tensor", wq, fw_ld)

            pf_free = [0, 0]
            psf_free = [0, 0]
            t1_free = [[0, 0], [0, 0]]
            g_free = [[0, 0], [0, 0]]
            t3_free = [[0, 0], [0, 0]]
            out_free = [0, 0]
            out_free4 = [0, 0, 0, 0]

            xld_at = [0] * (NT + 2)

            def issue_xload(i):
                xpar = i % 4
                b.wait("sync", st, out_free4[xpar])
                b.dma("sync", lambda e, i=i, xpar=xpar: e.dma_start(
                    out=x_sb[:, xpar * 1024:(xpar + 1) * 1024],
                    in_=x_hbm[i * 128:(i + 1) * 128, :]), xq)
                xld_at[i] = xq.n

            issue_xload(0)
            issue_xload(1)
            for i in range(NT):
                par = i % 2
                if i + 2 < NT:
                    issue_xload(i + 2)
                xld = xld_at[i]
                for nh in range(2):
                    b.wait("tensor", act, t1_free[par][nh])
                    b.wait("tensor", dve, pf_free[nh])
                    for c in range(8):
                        b.op("tensor", lambda e, c=c, i=i, nh=nh: e.matmul(
                            pf[:, nh * 512:(nh + 1) * 512],
                            xnT[:, c * 2048 + i * 128:c * 2048 + i * 128 + 128],
                            cbt_sb[:, c * 1024 + nh * 512:c * 1024 + nh * 512 + 512],
                            start=(c == 0), stop=False))
                    b.wait("tensor", dve, brow_n)
                    b.op("tensor", lambda e, nh=nh: e.matmul(
                        pf[:, nh * 512:(nh + 1) * 512], onesbf[:],
                        browbf[0:1, nh * 512:(nh + 1) * 512],
                        start=False, stop=True), inc=pe)
                    ffn_n = pe.n
                    b.wait("vector", pe, ffn_n)
                    b.wait("vector", dve, bcast_n)
                    b.wait("vector", act, t1_free[par][nh])
                    b.op("vector", lambda e, nh=nh, par=par: e.tensor_tensor(
                        out=t1[:, par * 1024 + nh * 512:par * 1024 + nh * 512 + 512],
                        in0=pf[:, nh * 512:(nh + 1) * 512],
                        in1=scb_bc[:, nh * 512:(nh + 1) * 512], op=ALU.mult), inc=dve)
                    pf_free[nh] = dve.n
                    t1n = dve.n
                    b.wait("scalar", dve, t1n)
                    b.wait("scalar", dve, g_free[par][nh])
                    b.op("scalar", lambda e, nh=nh, par=par: e.activation(
                        out=g_sb[:, par * 1024 + nh * 512:par * 1024 + nh * 512 + 512],
                        in_=t1[:, par * 1024 + nh * 512:par * 1024 + nh * 512 + 512],
                        func=AF.Silu), inc=act)
                    t1_free[par][nh] = act.n
                    gn = act.n
                    b.wait("tensor", dve, psf_free[nh])
                    for f in range(8):
                        b.op("tensor", lambda e, f=f, i=i, nh=nh: e.matmul(
                            psf[:, nh * 512:(nh + 1) * 512],
                            yT[:, f * 2048 + i * 128:f * 2048 + i * 128 + 128],
                            wo_sb[:, f * 1024 + nh * 512:f * 1024 + nh * 512 + 512],
                            start=(f == 0), stop=(f == 7)),
                            inc=(pe if f == 7 else None))
                    wo_n = pe.n
                    b.wait("vector", pe, wo_n)
                    b.wait("vector", act, gn)
                    b.wait("vector", pool, t3_free[par][nh])
                    b.op("vector", lambda e, nh=nh, par=par: e.tensor_tensor(
                        out=t3[:, par * 1024 + nh * 512:par * 1024 + nh * 512 + 512],
                        in0=psf[:, nh * 512:(nh + 1) * 512],
                        in1=g_sb[:, par * 1024 + nh * 512:par * 1024 + nh * 512 + 512],
                        op=ALU.mult), inc=dve)
                    psf_free[nh] = dve.n
                    g_free[par][nh] = dve.n
                    t3n = dve.n
                    b.wait("gpsimd", dve, t3n)
                    b.wait("gpsimd", xq, xld)
                    b.op("gpsimd", lambda e, nh=nh, par=par, xpar=i % 4: e.tensor_tensor(
                        out=out_sb[:, par * 1024 + nh * 512:par * 1024 + nh * 512 + 512],
                        in0=t3[:, par * 1024 + nh * 512:par * 1024 + nh * 512 + 512],
                        in1=x_sb[:, xpar * 1024 + nh * 512:xpar * 1024 + nh * 512 + 512],
                        op=ALU.add), inc=pool)
                    t3_free[par][nh] = pool.n
                outn = pool.n
                b.wait("sync", pool, outn)
                b.dma("sync", lambda e, i=i, par=par: e.dma_start(
                    out=y_hbm[i * 128:(i + 1) * 128, :],
                    in_=out_sb[:, par * 1024:(par + 1) * 1024]), st)
                out_free[par] = st.n
                out_free4[i % 4] = st.n

            b.wait("sync", st, st.n)
            flush()

    return nc


def _rope_scale_np():
    t = np.arange(T, dtype=np.float32)
    j = np.arange(0, DHR, 2, dtype=np.float32) / np.float32(DHR)
    inv = (np.float32(1.0) / np.power(np.float32(10000.0), j)).astype(np.float32)
    freqs = (t[:, None] * inv[None, :]).astype(np.float32)
    emb = np.concatenate([freqs, freqs], axis=-1)
    scale = (np.cos(emb) + np.sin(emb)).astype(np.float32)
    rep = np.ascontiguousarray(np.tile(scale.T, (4, 1)).astype(np.float32))
    return rep


def _masks_np():
    kk = np.arange(128)[:, None]
    qp = np.arange(512)[None, :]
    out = np.zeros((128, 2560), dtype=ml_dtypes.bfloat16)
    for r in range(4):
        out[:, r * 512:(r + 1) * 512] = np.where(
            qp < 128 * r + kk, np.float32(-1e5), np.float32(0)
        ).astype(ml_dtypes.bfloat16)
    return out


def make_inputs(inp):
    core_common = dict(
        W_DQ=np.ascontiguousarray(np.asarray(inp["L0_attn_W_DQ"], dtype=np.float32)),
        W_UQ=np.ascontiguousarray(np.asarray(inp["L0_attn_W_UQ"], dtype=np.float32)),
        W_DKV=np.ascontiguousarray(np.asarray(inp["L0_attn_W_DKV"], dtype=np.float32)),
        W_UK=np.ascontiguousarray(np.asarray(inp["L0_attn_W_UK"], dtype=np.float32)),
        W_UV=np.ascontiguousarray(np.asarray(inp["L0_attn_W_UV"], dtype=np.float32)),
        W_QR=np.ascontiguousarray(np.asarray(inp["L0_attn_W_QR"], dtype=np.float32)),
        W_KR=np.ascontiguousarray(np.asarray(inp["L0_attn_W_KR"], dtype=np.float32)),
        W_O=np.ascontiguousarray(np.asarray(inp["L0_attn_W_O"], dtype=np.float32)),
        CBT=np.ascontiguousarray(np.asarray(inp["L0_ffn_CB"], dtype=np.float32).T),
        SCB=np.ascontiguousarray(np.asarray(inp["L0_ffn_SCB"], dtype=np.float32).reshape(1, C)),
        BIAS=np.ascontiguousarray(np.asarray(inp["L0_ffn_bias"], dtype=np.float32).reshape(1, C)),
        ROPE=_rope_scale_np().astype(ml_dtypes.bfloat16),
        MASKS=_masks_np(),
        C127=np.full((1, 128), 1.0 / 127.0, dtype=np.float32),
        ONES=np.ones((1, 128), dtype=np.float32),
        ONESBF=np.ones((1, 128), dtype=ml_dtypes.bfloat16),
        IDENT=np.eye(128, dtype=ml_dtypes.bfloat16),
    )
    x = np.ascontiguousarray(np.asarray(inp["L0_attn_x"], dtype=np.float32))
    in_maps = []
    for bcore in range(8):
        m = dict(core_common)
        m["x"] = np.ascontiguousarray(x[bcore])
        in_maps.append(m)
    return in_maps


def kernel(**inputs):
    in_maps = make_inputs(inputs)
    nc = build_program()
    res = run_bass_kernel_spmd(nc, in_maps, core_ids=list(range(8)))
    return np.stack([r["y"] for r in res.results], axis=0)
